# version 04
# baseline (speedup 1.0000x reference)
import numpy as np
import ml_dtypes
import concourse.bass as bass
import concourse.mybir as mybir
from concourse.bass_utils import run_bass_kernel_spmd

F32 = mybir.dt.float32
BF16 = mybir.dt.bfloat16
AF = mybir.ActivationFunctionType
ALU = mybir.AluOpType
AX = mybir.AxisListType

MAGIC = 12582912.0
T, D, H, DH = 2048, 1024, 16, 64
I0, I1 = 2816, 4096
NCH, NI0, NI1 = 8, 22, 32
NTC, NTT = 4, 16
BFnp = ml_dtypes.bfloat16
C_LN127, C_SC0125, C_NEG1, C_NEGH, C_LN2_26 = 0, 1, 2, 3, 4
NCONST = 5
KB = 1024
DEBUG = False

TRI = []
_o = 0
for _i in range(NTT):
    TRI.append(_o)
    _o += (_i // 4 + 1) * 512
NTRI = _o


class Eng:
    def __init__(self, k, name, eng, step=1):
        self.k, self.name, self.eng, self.step = k, name, eng, step
        self.sem = k.nc.semaphore("sem_" + name).__enter__()
        self.cnt = 0

    def tag(self, instr, n=1):
        self.cnt += self.step * n
        instr.then_inc(self.sem, self.step * n)
        return self.cnt

    def wait(self, prod, at=None):
        at = prod.cnt if at is None else at
        if at <= 0:
            return
        key = (self.name, prod.name)
        if self.k.waited.get(key, 0) >= at:
            return
        self.eng.wait_ge(prod.sem, at)
        self.k.waited[key] = at


def build():
    k = type("K", (), {})()
    nc = bass.Bass("TRN2", target_bir_lowering=False)
    k.nc = nc
    k.waited = {}
    T_ = Eng(k, "T", nc.tensor)
    V = Eng(k, "V", nc.vector)
    S = Eng(k, "S", nc.scalar)
    G = Eng(k, "G", nc.gpsimd)
    LS = Eng(k, "LS", nc.sync, step=16)
    LA = Eng(k, "LA", nc.scalar, step=16)
    LG = Eng(k, "LG", nc.gpsimd, step=16)
    CC = Eng(k, "CC", nc.gpsimd)

    def barrier_LS():
        LS.wait(T_); LS.wait(V); LS.wait(S); LS.wait(G); LS.wait(LA)

    def barrier_LA():
        LA.wait(T_); LA.wait(V)

    def barrier_CMP():
        for E in (T_, V, S):
            E.wait(LS); E.wait(LA); E.wait(LG)

    DT = nc.dram_tensor
    xin = DT("x", [T, D], F32, kind="ExternalInput")
    out_h = DT("out", [T, D], F32, kind="ExternalOutput")
    wk_h = DT("wk", [D, D], F32, kind="ExternalInput")
    wq_h = DT("wq", [D, D], F32, kind="ExternalInput")
    wv_h = DT("wv", [D, D], F32, kind="ExternalInput")
    wo_h = DT("wo", [D, D], F32, kind="ExternalInput")
    wg_h = DT("wg", [I0, D], F32, kind="ExternalInput")
    wu_h = DT("wu", [I0, D], F32, kind="ExternalInput")
    wd_h = DT("wd", [D, I0], F32, kind="ExternalInput")
    wmod_h = [DT("wmod0", [D, 6 * D], F32, kind="ExternalInput"),
              DT("wmod1", [D, 6 * D], F32, kind="ExternalInput")]
    cb_h = [DT("cb0", [1, D], F32, kind="ExternalInput"),
            DT("cb1", [1, D], F32, kind="ExternalInput")]
    wqkvr_h = DT("wqkvr", [D, H * 256], F32, kind="ExternalInput")
    wv1_h = DT("wv1", [D, D], F32, kind="ExternalInput")
    biasr_h = DT("biasr_in", [64, 64], F32, kind="ExternalInput")
    bv_h = DT("bvT", [128, 8], F32, kind="ExternalInput")
    wout_h = DT("wout", [D, D], F32, kind="ExternalInput")
    bout_h = DT("boutT", [128, 8], F32, kind="ExternalInput")
    w1_h = DT("w1", [I1, D], F32, kind="ExternalInput")
    w2_h = DT("w2", [D, I1], F32, kind="ExternalInput")
    convw_h = DT("convwT", [128, NI1 * 3], F32, kind="ExternalInput")
    convb_h = DT("convbT", [128, NI1], F32, kind="ExternalInput")
    clipx_h = DT("clipx", [1, 1], F32, kind="ExternalInput")
    clipd_h = DT("clipd", [1, 1], F32, kind="ExternalInput")
    ident_h = DT("identh", [128, 128], F32, kind="ExternalInput")
    consts_h = DT("constsh", [128, NCONST], F32, kind="ExternalInput")
    cs64_h = DT("cs64h", [64, T], F32, kind="ExternalInput")
    sn64_h = DT("sn64h", [64, T], F32, kind="ExternalInput")
    dmask_h = DT("dmaskh", [128, 128], F32, kind="ExternalInput")

    xspill_h = DT("xspill", [128, NCH * T], F32, kind="Internal")
    wrow_h = DT("wrow", [I1, D], BF16, kind="Internal")
    wrow2_h = DT("wrow2", [D, I1], BF16, kind="Internal")
    wkT_h = DT("wkT_s", [D, D], BF16, kind="Internal")
    wqT_h = DT("wqT_s", [D, D], BF16, kind="Internal")
    wvT_h = DT("wvT_s", [D, D], BF16, kind="Internal")
    woT_h = DT("woT_s", [D, D], BF16, kind="Internal")
    wgT_h = DT("wgT_s", [D, I0], BF16, kind="Internal")
    wuT_h = DT("wuT_s", [D, I0], BF16, kind="Internal")
    wdT_h = DT("wdT_s", [I0, D], BF16, kind="Internal")
    w1T_h = DT("w1T_s", [D, I1], BF16, kind="Internal")
    w2T_h = DT("w2T_s", [I1, D], BF16, kind="Internal")
    prod_h = DT("prod_s", [I0, T], F32, kind="Internal")
    h1_h = DT("h1_s", [I1, T], F32, kind="Internal")
    y_h = DT("y_s", [I1, T], F32, kind="Internal")
    ccin_h = [DT(f"cc_in{i}", [1, 1], F32, kind="Internal") for i in range(2)]
    ccout_h = [DT(f"cc_out{i}", [1, 1], F32, kind="Internal") for i in range(2)]

    A = nc.alloc_sbuf_tensor
    ATo = nc.alloc_sbuf_tensor_at

    tscr = A("tscr", [128, 2048], F32)
    tscr2 = A("tscr2", [128, 2048], F32)
    ident = A("ident", [128, 128], F32)
    dmask = A("dmask", [128, 128], F32)
    consts = A("consts", [128, NCONST], F32)
    onesc = A("onesc", [128, 1], F32)
    onesc_b = A("onesc_b", [128, 1], BF16)
    ones1r = A("ones1r", [1, 128], F32)
    modv = {}
    for l in (0, 1):
        for nm in ("sa", "ca1", "ga", "sb", "cb1", "gb"):
            modv[(l, nm)] = A(f"mod_{l}_{nm}", [128, NCH], F32)
    swg = A("swg", [128, NI0], F32)
    swu = A("swu", [128, NI0], F32)
    swd = A("swd", [128, NCH], F32)
    swg_s = A("swg_s", [128, NI0], F32)
    swu_s = A("swu_s", [128, NI0], F32)
    swd_s = A("swd_s", [128, NCH], F32)
    swsc = A("swsc", [1, 4], F32)
    biasr = A("biasr", [64, 64], F32)
    bv = A("bv", [128, 8], F32)
    bout = A("bout", [128, 8], F32)
    convw = A("convw", [128, NI1 * 3], F32)
    convb = A("convb", [128, NI1], F32)
    clip_sb = A("clip_sb", [1, 2], F32)
    gsc = A("gsc", [1, 8], F32)
    c_row = A("c_row", [1, 2 * D], F32)
    scT_f = A("scT_f", [128, 2 * NCH], F32)
    scT_b = A("scT_b", [128, 2 * NCH], BF16)
    mod_raw = A("mod_raw", [128, 2 * 48], F32)
    modw_f = A("modw_f", [128, 2 * 1024], F32)
    modw_b = A("modw_b", [128, 2 * 1024], BF16)
    sc1 = A("sc1", [128, 1], F32)
    sc2 = A("sc2", [128, 1], F32)
    sc3 = A("sc3", [128, 1], F32)
    sc4 = A("sc4", [128, 1], F32)
    racc = A("racc", [128, 16], F32)
    rinv = A("rinv", [128, 16], F32)
    kn2c = A("kn2c", [128, 16], F32)
    mhc = A("mhc", [128, 16], F32)
    tmax = A("tmax", [128, 16], F32)
    gab = A("gab", [128, 8], F32)
    cqcol = A("cqcol", [128, NCH], F32)
    sqcol = A("sqcol", [128, NCH], F32)
    vt_bf = A("vt_bf", [128, 16 * 64], BF16)
    s1b = A("s1b", [128, 512], F32)

    x_sb = A("x_sb", [128, NCH * T], F32)
    hb = A("hb", [128, NCH * T], BF16)
    v_sb = A("v_sb", [128, 16 * 1040], BF16)
    z_sb = A("z_sb", [128, NCH * T], BF16)

    def addr(t):
        return nc.lookup_mloc(t).addr

    pa = addr(x_sb)
    at_bf = ATo("at_bf", [128, NTRI], BF16, offset=pa)
    nsq_a = ATo("nsq_a", [64, 2048], F32, offset=pa + 48 * KB)
    nsq_b = ATo("nsq_b", [64, 2048], F32, offset=pa + 56 * KB)
    sl1_bf = ATo("sl1_bf", [128, 16 * 512], BF16, offset=pa)
    cs64 = ATo("cs64", [64, 2048], F32, offset=pa + 16 * KB)
    sn64 = ATo("sn64", [64, 2048], F32, offset=pa + 24 * KB)
    nsq6 = ATo("nsq6", [64, 2048], F32, offset=pa + 32 * KB)
    rope_f = ATo("rope_f", [64, 2048], F32, offset=pa + 48 * KB)
    rope_g = ATo("rope_g", [64, 2048], F32, offset=pa + 56 * KB)
    kh_bf = ATo("kh_bf", [64, 2048], BF16, offset=pa + 40 * KB)
    qh_bf = ATo("qh_bf", [64, 2048], BF16, offset=pa + 44 * KB)

    pc = addr(v_sb)
    iq2t_bf = ATo("iq2t_bf", [128, NI0 * 512], BF16, offset=pc)
    prodt_f = ATo("prodt_f", [128, 1024], F32, offset=pc + 22 * KB)
    iq2_bf = ATo("iq2_bf", [128, NI1 * 512], BF16, offset=pc)

    pb = addr(hb)
    outsb = ATo("outsb", [128, 2048], F32, offset=pb)

    ps2a = addr(tscr2)
    wbuf = ATo("wbuf", [128, 2048], BF16, offset=ps2a)
    xbb = ATo("xbb", [128, 2048], BF16, offset=ps2a + 4 * KB)
    wfp = tscr

    pd = addr(z_sb)
    tern_scr = ATo("tern_scr", [128, 2048], F32, offset=pd)
    wstr_a = ATo("wstr_a", [128, 2048], BF16, offset=pd + 8 * KB)
    wstr_b = ATo("wstr_b", [128, 2048], BF16, offset=pd + 12 * KB)
    rowst = ATo("rowst", [1, 2048], F32, offset=pd + 16 * KB)
    r1row = ATo("r1row", [1, 2048], F32, offset=pd + 24 * KB)

    ps = [nc.alloc_psum_tensor(f"ps{i}", [128, 512], F32) for i in range(8)]
    ps_cons = [[] for _ in range(8)]

    def ps_wait(bank):
        for (q, c) in ps_cons[bank]:
            T_.wait(q, c)
        ps_cons[bank] = []

    def ps_done(bank, q, cnt=None):
        ps_cons[bank].append((q, cnt if cnt is not None else q.cnt))

    def load(dst, src, q=LS):
        return q.tag(q.eng.dma_start(dst, src))

    def loadT(dst, src):
        return LS.tag(nc.sync.dma_start(dst, src, transpose=True))

    def store(dst, src):
        return LG.tag(nc.gpsimd.dma_start(dst, src))

    MM = nc.tensor.matmul

    dbg_list = []

    def dbg(name, ap):
        if not DEBUG:
            return
        th = DT("d_" + name, list(ap.shape), ap.dtype, kind="ExternalOutput")
        G.wait(V)
        G.wait(S)
        G.wait(T_)
        store(th[tuple(slice(0, n) for n in ap.shape)], ap)
        V.wait(LG)
        S.wait(LG)
        T_.wait(LG)
        LS.wait(LG)
        LA.wait(LG)
        dbg_list.append(name)
    nc._dbg_list = dbg_list

    load(ident[:, :], ident_h[:, :])
    load(dmask[:, :], dmask_h[:, :])
    load(consts[:, :], consts_h[:, :])
    load(biasr[:, :], biasr_h[:, :])
    load(bv[:, :], bv_h[:, :])
    load(bout[:, :], bout_h[:, :])
    load(convw[:, :], convw_h[:, :])
    load(convb[:, :], convb_h[:, :])
    load(clip_sb[0:1, 0:1], clipx_h[:, :])
    load(clip_sb[0:1, 1:2], clipd_h[:, :])
    load(c_row[0:1, 0:D], cb_h[0][:, :])
    m_base = load(c_row[0:1, D:2 * D], cb_h[1][:, :])
    V.wait(LS, m_base)
    V.tag(nc.vector.memset(onesc[:, :], 1.0))
    V.tag(nc.vector.memset(onesc_b[:, :], 1.0))
    V.tag(nc.vector.memset(ones1r[:, :], 1.0))

    for tt in range(NTT):
        buf = tscr if tt % 2 == 0 else tscr2
        mk = load(buf[:, 0:1024], xin[tt * 128:(tt + 1) * 128, :], q=LA)
        T_.wait(LA, mk)
        T_.wait(LS, m_base)
        for dc in range(NCH):
            bank = 4 + (dc % 2)
            ps_wait(bank)
            T_.tag(nc.tensor.transpose(ps[bank][:, 0:128],
                                       buf[:, dc * 128:(dc + 1) * 128],
                                       ident[:, :]))
            V.wait(T_)
            cm = V.tag(nc.vector.tensor_copy(
                x_sb[:, dc * T + tt * 128: dc * T + (tt + 1) * 128],
                ps[bank][:, 0:128]))
            ps_done(bank, V, cm)
        barrier_LA()

    S.wait(LS, m_base)
    sm = S.tag(nc.scalar.activation(c_row[:, :], c_row[:, :], AF.Silu))
    T_.wait(S, sm)
    T_.wait(V)
    for l in (0, 1):
        for dc in range(NCH):
            bank = 6
            ps_wait(bank)
            T_.tag(MM(ps[bank][:, 0:1],
                      c_row[0:1, l * D + dc * 128: l * D + (dc + 1) * 128],
                      onesc[0:1, 0:1], start=True, stop=True))
            V.wait(T_)
            cm = V.tag(nc.vector.tensor_copy(
                scT_b[:, l * NCH + dc: l * NCH + dc + 1], ps[bank][:, 0:1]))
            ps_done(bank, V, cm)
    dbg("scTb_early", scT_b[:, :])

    for l in (0, 1):
        if l == 1:
            dbg("scTb_mid", scT_b[:, :])
        for j in range(48):
            half = (j % 2) * 1024
            buf = modw_f[:, half:half + 1024]
            bb = modw_b[:, half:half + 1024]
            barrier_LA()
            mk = 0
            for dc in range(NCH):
                mk = load(buf[:, dc * 128:(dc + 1) * 128],
                          wmod_h[l][dc * 128:(dc + 1) * 128,
                                    j * 128:(j + 1) * 128], q=LA)
            V.wait(LA, mk)
            V.wait(T_)
            vm = V.tag(nc.vector.tensor_copy(bb[:, 0:1024], buf[:, 0:1024]))
            bank = 6 + (j % 2)
            ps_wait(bank)
            T_.wait(V, vm)
            for dc in range(NCH):
                T_.tag(MM(ps[bank][:, 0:1], bb[:, dc * 128:(dc + 1) * 128],
                          scT_b[:, l * NCH + dc: l * NCH + dc + 1],
                          start=(dc == 0), stop=(dc == NCH - 1)))
            V.wait(T_)
            cm = V.tag(nc.vector.tensor_copy(
                mod_raw[:, l * 48 + j: l * 48 + j + 1], ps[bank][:, 0:1]))
            ps_done(bank, V, cm)
    for l in (0, 1):
        for vi, nm in enumerate(("sa", "ca1", "ga", "sb", "cb1", "gb")):
            src = mod_raw[:, l * 48 + vi * 8: l * 48 + (vi + 1) * 8]
            dst = modv[(l, nm)]
            if nm in ("ga", "gb"):
                V.tag(nc.vector.tensor_scalar(gab[:, 0:8], src, 0.0, 2.0,
                                              op0=ALU.is_ge, op1=ALU.mult))
                V.wait(V)
                V.tag(nc.vector.tensor_scalar_add(gab[:, 0:8], gab[:, 0:8],
                                                  -1.0))
                V.wait(V)
                vv = V.cnt
                S.wait(V, vv)
                sm = S.tag(nc.scalar.activation(scT_f[:, 0:8], src, AF.Abs))
                V.wait(S, sm)
                V.tag(nc.vector.scalar_tensor_tensor(
                    dst[:, :], scT_f[:, 0:8], 0.25, gab[:, 0:8],
                    op0=ALU.max, op1=ALU.mult))
            elif nm in ("ca1", "cb1"):
                V.tag(nc.vector.tensor_scalar_add(dst[:, :], src, 1.0))
            else:
                V.tag(nc.vector.tensor_copy(dst[:, :], src))
    mod_ready = V.cnt
    dbg("mod_raw", mod_raw[:, :])
    dbg("scTb", scT_b[:, :])
    dbg("x0", x_sb[:, :])

    def prep_cast(src_h, R, C, row_h, dstT_h):
        barrier_LS()
        for r in range(R // 128):
            mk = load(wfp[:, 0:C], src_h[r * 128:(r + 1) * 128, :])
            V.wait(LS, mk)
            V.wait(LG)
            vm = V.tag(nc.vector.tensor_copy(wbuf[:, 0:C], wfp[:, 0:C]))
            G.wait(V, vm)
            store(row_h[r * 128:(r + 1) * 128, 0:C], wbuf[:, 0:C])
            LS.wait(V, vm)
        st = LG.cnt
        LS.wait(LG, st)
        for e in range(C // 128):
            for r0 in range(0, R, 2048):
                rw = min(2048, R - r0)
                mk = loadT(xbb[:, 0:rw],
                           row_h[r0:r0 + rw, e * 128:(e + 1) * 128])
                G.wait(LS, mk)
                store(dstT_h[e * 128:(e + 1) * 128, r0:r0 + rw], xbb[:, 0:rw])
                LS.wait(LG)
        LS.wait(LG)

    def prep_quant8(src_h, R, C, row_h, dstT_h, scale_sb):
        barrier_LS()
        pieces = [(0, C)] if C <= 2048 else [(0, 2048), (2048, C - 2048)]
        for r in range(R // 128):
            for pi, (c0, w) in enumerate(pieces):
                mk = load(wfp[:, 0:w], src_h[r * 128:(r + 1) * 128, c0:c0 + w])
                V.wait(LS, mk)
                vm = V.tag(nc.vector.reduce_max(
                    sc1[:, :] if pi == 0 else sc2[:, :], wfp[:, 0:w],
                    axis=AX.X, apply_absolute_value=True))
                if pi == 1:
                    V.wait(V)
                    vm = V.tag(nc.vector.tensor_max(sc1[:, :], sc1[:, :],
                                                    sc2[:, :]))
                LS.wait(V, vm)
            vm0 = V.cnt
            S.wait(V, vm0)
            S.tag(nc.scalar.activation(sc3[:, :], sc1[:, :], AF.Ln))
            S.wait(S)
            sm = S.tag(nc.scalar.activation(sc4[:, :], sc3[:, :], AF.Exp,
                                            bias=consts[:, C_LN127:C_LN127 + 1],
                                            scale=consts[:, C_NEG1:C_NEG1 + 1]))
            S.wait(S, sm)
            V.wait(S, sm)
            V.tag(nc.vector.tensor_scalar_mul(scale_sb[:, r:r + 1], sc1[:, :],
                                              1.0 / 127.0))
            for (c0, w) in pieces:
                mk = load(wfp[:, 0:w], src_h[r * 128:(r + 1) * 128, c0:c0 + w])
                S.wait(LS, mk)
                sm2 = S.tag(nc.scalar.activation(wfp[:, 0:w], wfp[:, 0:w],
                                                 AF.Copy, bias=0.0,
                                                 scale=sc4[:, 0:1]))
                V.wait(S, sm2)
                V.tag(nc.vector.tensor_scalar(wfp[:, 0:w], wfp[:, 0:w], -128.0,
                                              127.0, op0=ALU.max, op1=ALU.min))
                V.wait(LG)
                vm = V.tag(nc.vector.tensor_scalar(wbuf[:, 0:w], wfp[:, 0:w],
                                                   MAGIC, MAGIC, op0=ALU.add,
                                                   op1=ALU.subtract))
                G.wait(V, vm)
                store(row_h[r * 128:(r + 1) * 128, c0:c0 + w], wbuf[:, 0:w])
                LS.wait(V, vm)
        st = LG.cnt
        LS.wait(LG, st)
        for e in range(C // 128):
            for r0 in range(0, R, 2048):
                rw = min(2048, R - r0)
                mk = loadT(xbb[:, 0:rw],
                           row_h[r0:r0 + rw, e * 128:(e + 1) * 128])
                G.wait(LS, mk)
                store(dstT_h[e * 128:(e + 1) * 128, r0:r0 + rw], xbb[:, 0:rw])
                LS.wait(LG)
        LS.wait(LG)

    def prep_ternary(src_h, R, C, row_h, dstT_h, sw_slot):
        barrier_LS()
        pieces = [(0, C)] if C <= 2048 else [(0, 2048), (2048, C - 2048)]
        V.tag(nc.vector.memset(sc1[:, :], 0.0))
        for r in range(R // 128):
            for (c0, w) in pieces:
                mk = load(wfp[:, 0:w], src_h[r * 128:(r + 1) * 128, c0:c0 + w])
                V.wait(LS, mk)
                V.tag(nc.vector.reduce_sum(sc2[:, :], wfp[:, 0:w], axis=AX.X,
                                           apply_absolute_value=True))
                V.wait(V)
                vm = V.tag(nc.vector.tensor_add(sc1[:, :], sc1[:, :],
                                                sc2[:, :]))
                LS.wait(V, vm)
        bank = 6
        ps_wait(bank)
        T_.wait(V)
        T_.tag(MM(ps[bank][0:1, 0:1], sc1[:, :], onesc[:, :], start=True,
                  stop=True))
        V.wait(T_)
        V.tag(nc.vector.tensor_scalar_mul(gsc[0:1, 0:1], ps[bank][0:1, 0:1],
                                          1.0 / (R * C)))
        V.wait(V)
        V.tag(nc.vector.tensor_scalar_max(gsc[0:1, 0:1], gsc[0:1, 0:1], 1e-5))
        V.wait(V)
        vm = V.tag(nc.vector.tensor_scalar_mul(swsc[0:1, sw_slot:sw_slot + 1],
                                               gsc[0:1, 0:1], -1.0))
        ps_done(bank, V, vm)
        S.wait(V, vm)
        S.tag(nc.scalar.activation(gsc[0:1, 1:2], gsc[0:1, 0:1], AF.Ln))
        S.wait(S)
        sm = S.tag(nc.scalar.activation(gsc[0:1, 2:3], gsc[0:1, 1:2], AF.Exp,
                                        bias=0.0,
                                        scale=consts[0:1, C_NEG1:C_NEG1 + 1]))
        bank = 7
        ps_wait(bank)
        T_.wait(S, sm)
        T_.tag(MM(ps[bank][:, 0:1], ones1r[:, :], gsc[0:1, 2:3], start=True,
                  stop=True))
        V.wait(T_)
        vm = V.tag(nc.vector.tensor_copy(sc3[:, :], ps[bank][:, 0:1]))
        ps_done(bank, V, vm)
        S.wait(V, vm)
        for r in range(R // 128):
            for (c0, w) in pieces:
                mk = load(wfp[:, 0:w], src_h[r * 128:(r + 1) * 128, c0:c0 + w])
                S.wait(LS, mk)
                sm = S.tag(nc.scalar.activation(wfp[:, 0:w], wfp[:, 0:w],
                                                AF.Copy, bias=0.0,
                                                scale=sc3[:, 0:1]))
                V.wait(S, sm)
                V.tag(nc.vector.tensor_single_scalar(tern_scr[:, 0:w],
                                                     wfp[:, 0:w], 0.5,
                                                     op=ALU.is_ge))
                V.wait(LG)
                vm = V.tag(nc.vector.scalar_tensor_tensor(
                    wbuf[:, 0:w], wfp[:, 0:w], -0.5, tern_scr[:, 0:w],
                    op0=ALU.is_le, op1=ALU.subtract))
                G.wait(V, vm)
                store(row_h[r * 128:(r + 1) * 128, c0:c0 + w], wbuf[:, 0:w])
                LS.wait(V, vm)
        st = LG.cnt
        LS.wait(LG, st)
        for e in range(C // 128):
            for r0 in range(0, R, 2048):
                rw = min(2048, R - r0)
                mk = loadT(xbb[:, 0:rw],
                           row_h[r0:r0 + rw, e * 128:(e + 1) * 128])
                G.wait(LS, mk)
                store(dstT_h[e * 128:(e + 1) * 128, r0:r0 + rw], xbb[:, 0:rw])
                LS.wait(LG)
        LS.wait(LG)

    prep_cast(wk_h, D, D, wrow_h, wkT_h)
    prep_cast(wq_h, D, D, wrow_h, wqT_h)
    prep_cast(wv_h, D, D, wrow_h, wvT_h)
    prep_cast(wo_h, D, D, wrow_h, woT_h)

    def ln_prep_t(src_sb, t):
        ps_wait(6)
        ps_wait(7)
        S.wait(T_)
        S.wait(V)
        for dc in range(NCH):
            sm = S.tag(nc.scalar.activation(
                tscr[:, 512:1024],
                src_sb[:, dc * T + t * 512: dc * T + (t + 1) * 512], AF.Square))
            T_.wait(S, sm)
            T_.tag(MM(ps[7][0:1, 0:512], onesc[:, :], tscr[:, 512:1024],
                      start=(dc == 0), stop=(dc == NCH - 1)))
            T_.tag(MM(ps[6][0:1, 0:512], onesc[:, :],
                      src_sb[:, dc * T + t * 512: dc * T + (t + 1) * 512],
                      start=(dc == 0), stop=(dc == NCH - 1)))
            if dc < NCH - 1:
                S.wait(T_)
        V.wait(T_)
        V.tag(nc.vector.tensor_scalar_mul(rowst[0:1, 0:512], ps[6][0:1, 0:512],
                                          1.0 / D))
        vm = V.tag(nc.vector.tensor_scalar_mul(rowst[0:1, 1536:2048],
                                               ps[7][0:1, 0:512], 1.0 / D))
        ps_done(6, V, vm)
        ps_done(7, V, vm)
        V.tag(nc.vector.tensor_mul(rowst[0:1, 512:1024], rowst[0:1, 0:512],
                                   rowst[0:1, 0:512]))
        V.tag(nc.vector.tensor_sub(rowst[0:1, 1536:2048],
                                   rowst[0:1, 1536:2048],
                                   rowst[0:1, 512:1024]))
        vm = V.tag(nc.vector.tensor_scalar_add(rowst[0:1, 1536:2048],
                                               rowst[0:1, 1536:2048], 1e-5))
        S.wait(V, vm)
        S.tag(nc.scalar.activation(rowst[0:1, 512:1024], rowst[0:1, 1536:2048],
                                   AF.Ln))
        S.wait(S)
        sm = S.tag(nc.scalar.activation(rowst[0:1, 512:1024],
                                        rowst[0:1, 512:1024], AF.Exp, bias=0.0,
                                        scale=consts[0:1, C_NEGH:C_NEGH + 1]))
        V.wait(S, sm)
        vm = V.tag(nc.vector.tensor_mul(rowst[0:1, 1024:1536],
                                        rowst[0:1, 0:512],
                                        rowst[0:1, 512:1024]))
        ps_wait(4)
        ps_wait(5)
        T_.wait(V, vm)
        T_.tag(MM(ps[4][:, 0:512], ones1r[:, :], rowst[0:1, 512:1024],
                  start=True, stop=True))
        T_.tag(MM(ps[5][:, 0:512], ones1r[:, :], rowst[0:1, 1024:1536],
                  start=True, stop=True))

    def ln_u(dst, src_sb, dc, t):
        V.wait(T_)
        V.tag(nc.vector.tensor_mul(
            dst, src_sb[:, dc * T + t * 512: dc * T + (t + 1) * 512],
            ps[4][:, 0:512]))
        return V.tag(nc.vector.tensor_sub(dst, dst, ps[5][:, 0:512]))

    V.wait(V, mod_ready)
    barrier_CMP()
    for t in range(NTC):
        ln_prep_t(x_sb, t)
        vm = 0
        for dc in range(NCH):
            vm = ln_u(tscr2[:, 0:512], x_sb, dc, t)
            S.wait(V, vm)
            sm = S.tag(nc.scalar.activation(
                hb[:, dc * T + t * 512: dc * T + (t + 1) * 512],
                tscr2[:, 0:512], AF.Identity,
                bias=modv[(0, "sa")][:, dc:dc + 1],
                scale=modv[(0, "ca1")][:, dc:dc + 1]))
            V.wait(S, sm)
        ps_done(4, V, vm)
        ps_done(5, V, vm)
    h_ready = S.cnt
    dbg("h_attn0", hb[:, :])

    G.wait(S, h_ready)
    G.wait(V)
    for dc in range(NCH):
        store(xspill_h[:, dc * T:(dc + 1) * T], x_sb[:, dc * T:(dc + 1) * T])
    xsp_done = LG.cnt

    barrier_LA()
    for ct in range(NTT):
        for nf in range(2):
            bank = nf
            ps_wait(bank)
            for dc in range(NCH):
                rb = wbuf if (dc % 2 == 0) else xbb
                mk = load(rb[:, 0:512],
                          wvT_h[dc * 128:(dc + 1) * 128,
                                nf * 512:(nf + 1) * 512], q=LA)
                T_.wait(LA, mk)
                T_.wait(S, h_ready)
                T_.tag(MM(ps[bank][:, 0:512],
                          hb[:, dc * T + ct * 128: dc * T + (ct + 1) * 128],
                          rb[:, 0:512], start=(dc == 0), stop=(dc == NCH - 1)))
                barrier_LA()
            V.wait(T_)
            cm = V.tag(nc.vector.tensor_copy(
                v_sb[:, ct * 1040 + nf * 512: ct * 1040 + nf * 512 + 512],
                ps[bank][:, 0:512]))
            ps_done(bank, V, cm)
    v_ready = V.cnt
    dbg("v0", v_sb[:, :])

    V.wait(LG, xsp_done)
    S.wait(LG, xsp_done)

    for h in range(H):
        par = h % 2
        for (mat_h, dst) in ((wkT_h, kh_bf), (wqT_h, qh_bf)):
            for t in range(NTC):
                bank = 2 + (t % 2)
                ps_wait(bank)
                for dc in range(NCH):
                    rb = wbuf if dc % 2 == 0 else xbb
                    mk = load(rb[:, 0:64],
                              mat_h[dc * 128:(dc + 1) * 128,
                                    h * 64:(h + 1) * 64], q=LA)
                    T_.wait(LA, mk)
                    T_.tag(MM(ps[bank][0:64, 0:512], rb[:, 0:64],
                              hb[:, dc * T + t * 512: dc * T + (t + 1) * 512],
                              start=(dc == 0), stop=(dc == NCH - 1)))
                    barrier_LA()
                V.wait(T_)
                cm = V.tag(nc.vector.tensor_copy(dst[:, t * 512:(t + 1) * 512],
                                                 ps[bank][0:64, 0:512]))
                ps_done(bank, V, cm)
        kq_ready = V.cnt

        S.wait(V, kq_ready)
        S.tag(nc.scalar.activation(nsq_a[:, :], kh_bf[:, :], AF.Square))
        sm2 = S.tag(nc.scalar.activation(nsq_b[:, :], qh_bf[:, :], AF.Square))
        T_.wait(S, sm2)
        V.tag(nc.vector.memset(sc1[:, :], 0.0))
        for ct in range(NTT):
            bank = 6 + (ct % 2)
            ps_wait(bank)
            T_.tag(MM(ps[bank][:, 0:1], nsq_a[:, ct * 128:(ct + 1) * 128],
                      onesc[0:64, :], start=True, stop=True))
            V.wait(T_)
            cm = V.tag(nc.vector.tensor_copy(kn2c[:, ct:ct + 1],
                                             ps[bank][:, 0:1]))
            ps_done(bank, V, cm)
        for ct in range(NTT):
            bank = 6 + (ct % 2)
            ps_wait(bank)
            T_.tag(MM(ps[bank][:, 0:1], nsq_b[:, ct * 128:(ct + 1) * 128],
                      onesc[0:64, :], start=True, stop=True))
            V.wait(T_)
            V.wait(V)
            cm = V.tag(nc.vector.tensor_max(sc1[:, :], sc1[:, :],
                                            ps[bank][:, 0:1]))
            ps_done(bank, V, cm)
        bank = 6
        ps_wait(bank)
        T_.wait(V)
        T_.tag(nc.tensor.transpose(ps[bank][0:1, 0:128], sc1[:, 0:1],
                                   ident[:, :]))
        V.wait(T_)
        vm = V.tag(nc.vector.reduce_max(gsc[0:1, 0:1], ps[bank][0:1, 0:128],
                                        axis=AX.X))
        ps_done(bank, V, vm)
        S.wait(V, vm)
        sm = S.tag(nc.scalar.activation(gsc[0:1, 1:2], gsc[0:1, 0:1], AF.Sqrt))
        bank = 7
        ps_wait(bank)
        T_.wait(S, sm)
        T_.tag(MM(ps[bank][:, 0:1], ones1r[:, :], gsc[0:1, 1:2], start=True,
                  stop=True))
        V.wait(T_)
        vm = V.tag(nc.vector.tensor_scalar_mul(sc2[:, :], ps[bank][:, 0:1],
                                               -0.125 * 1.0001))
        ps_done(bank, V, vm)
        S.wait(V, vm)
        sm = S.tag(nc.scalar.activation(mhc[:, :], kn2c[:, :], AF.Sqrt))
        V.wait(S, sm)
        V.wait(V)
        mh_done = V.tag(nc.vector.tensor_scalar(mhc[:, :], mhc[:, :],
                                                sc2[:, 0:1], None,
                                                op0=ALU.mult))
        S.wait(V, mh_done)

        V.tag(nc.vector.memset(racc[:, :], 0.0))
        for i in range(NTT):
            nj = i // 4 + 1
            for j in range(nj):
                bank = (i + j) % 2
                ps_wait(bank)
                T_.wait(V, kq_ready)
                wj = 512 if j < nj - 1 else (i % 4 + 1) * 128
                T_.tag(MM(ps[bank][0:128, 0:wj],
                          kh_bf[:, i * 128:(i + 1) * 128],
                          qh_bf[:, j * 512: j * 512 + wj], start=True,
                          stop=True))
                diag = (j == nj - 1)
                if diag:
                    sub = i % 4
                    V.wait(T_)
                    V.tag(nc.vector.tensor_add(
                        ps[bank][:, sub * 128:(sub + 1) * 128],
                        ps[bank][:, sub * 128:(sub + 1) * 128], dmask[:, :]))
                S.wait(T_)
                S.wait(V)
                sm = S.tag(nc.scalar.activation(
                    at_bf[:, TRI[i] + j * 512: TRI[i] + j * 512 + wj],
                    ps[bank][:, 0:wj], AF.Exp, bias=mhc[:, i:i + 1],
                    scale=consts[:, C_SC0125:C_SC0125 + 1],
                    accum_out=sc1[:, 0:1]))
                ps_done(bank, S, sm)
                V.wait(S, sm)
                V.tag(nc.vector.tensor_add(racc[:, i:i + 1], racc[:, i:i + 1],
                                           sc1[:, 0:1]))
                if wj < 512:
                    V.tag(nc.vector.memset(
                        at_bf[:, TRI[i] + j * 512 + wj: TRI[i] + (j + 1) * 512],
                        0.0))
        V.wait(V)
        vm = V.tag(nc.vector.tensor_scalar(rinv[:, :], racc[:, :],
                                           2.0 ** -96, 2.0 ** 26,
                                           op0=ALU.max, op1=ALU.mult))
        S.wait(V, vm)
        S.tag(nc.scalar.activation(rinv[:, :], rinv[:, :], AF.Ln))
        S.wait(S)
        sm = S.tag(nc.scalar.activation(rinv[:, :], rinv[:, :], AF.Exp,
                                        bias=consts[:, C_LN2_26:C_LN2_26 + 1],
                                        scale=consts[:, C_NEG1:C_NEG1 + 1]))
        V.wait(S, sm)
        V.wait(V, v_ready)
        for ct in range(NTT):
            V.tag(nc.vector.tensor_scalar(
                vt_bf[:, ct * 64:(ct + 1) * 64],
                v_sb[:, ct * 1040 + h * 64: ct * 1040 + h * 64 + 64],
                rinv[:, ct:ct + 1], None, op0=ALU.mult))
        vt_done = V.cnt
        if h == 0:
            dbg("at_h0", at_bf[:, :])
            dbg("vt_h0", vt_bf[:, :])
            dbg("racc_h0", racc[:, :])
            dbg("rinv_h0", rinv[:, :])
            dbg("mhc_h0", mhc[:, :])

        for j in range(NTC):
            bank = 2 + (j % 2)
            ps_wait(bank)
            T_.wait(V, vt_done)
            T_.wait(G) if (h == 0 and DEBUG) else None
            tp = None if par == 0 else (0, 64)
            rows = slice(0, 64) if par == 0 else slice(64, 128)
            for i in range(4 * j, NTT):
                T_.tag(MM(ps[bank][rows, 0:512], vt_bf[:, i * 64:(i + 1) * 64],
                          at_bf[:, TRI[i] + j * 512: TRI[i] + (j + 1) * 512],
                          start=(i == 4 * j), stop=(i == NTT - 1),
                          tile_position=tp))
            V.wait(T_)
            cm = V.tag(nc.vector.tensor_copy(
                z_sb[rows, (h // 2) * T + j * 512: (h // 2) * T + (j + 1) * 512],
                ps[bank][rows, 0:512]))
            ps_done(bank, V, cm)
    z_ready = V.cnt
    dbg("z0", z_sb[:, :])
    dbg("kh15", kh_bf[:, :])
    dbg("qh15", qh_bf[:, :])
    dbg("racc15", racc[:, :])
    dbg("mhc15", mhc[:, :])

    LS.wait(T_)
    LS.wait(V, z_ready)
    LS.wait(S)
    barrier_LA()
    dbg("ga0v", modv[(0, "ga")][:, :])
    first_ps = [True]
    for e in range(NCH):
        mk = load(x_sb[:, e * T:(e + 1) * T], xspill_h[:, e * T:(e + 1) * T])
        for t in range(NTC):
            bank = t % 2
            ps_wait(bank)
            for fc in range(NCH):
                rb = wbuf if fc % 2 == 0 else xbb
                mw = load(rb[:, 0:128],
                          woT_h[fc * 128:(fc + 1) * 128, e * 128:(e + 1) * 128],
                          q=LA)
                T_.wait(LA, mw)
                T_.wait(V, z_ready)
                T_.tag(MM(ps[bank][:, 0:512], rb[:, 0:128],
                          z_sb[:, fc * T + t * 512: fc * T + (t + 1) * 512],
                          start=(fc == 0), stop=(fc == NCH - 1)))
                barrier_LA()
            V.wait(T_)
            if first_ps[0] and DEBUG:
                first_ps[0] = False
                vv_ = V.tag(nc.vector.tensor_copy(tscr[:, 0:512],
                                                  ps[bank][:, 0:512]))
                dbg("oproj_ps00", tscr[:, 0:512])
            V.wait(LS, mk)
            cm = V.tag(nc.vector.scalar_tensor_tensor(
                x_sb[:, e * T + t * 512: e * T + (t + 1) * 512],
                ps[bank][:, 0:512], modv[(0, "ga")][:, e:e + 1],
                x_sb[:, e * T + t * 512: e * T + (t + 1) * 512],
                op0=ALU.mult, op1=ALU.add))
            ps_done(bank, V, cm)
    x_attn0 = V.cnt
    dbg("xa0", x_sb[:, :])

    prep_quant8(wg_h, I0, D, wrow_h, wgT_h, swg)
    prep_quant8(wu_h, I0, D, wrow_h, wuT_h, swu)
    prep_quant8(wd_h, D, I0, wrow2_h, wdT_h, swd)

    V.wait(V, x_attn0)
    barrier_CMP()
    V.tag(nc.vector.memset(sc1[:, :], 0.0))
    for t in range(NTC):
        ln_prep_t(x_sb, t)
        vm = 0
        for dc in range(NCH):
            vm = ln_u(tscr2[:, 0:512], x_sb, dc, t)
            S.wait(V, vm)
            sm = S.tag(nc.scalar.activation(tscr[:, 0:512], tscr2[:, 0:512],
                                            AF.Identity,
                                            bias=modv[(0, "sb")][:, dc:dc + 1],
                                            scale=modv[(0, "cb1")][:, dc:dc + 1]))
            V.wait(S, sm)
            V.tag(nc.vector.reduce_max(sc2[:, :], tscr[:, 0:512], axis=AX.X,
                                       apply_absolute_value=True))
            V.wait(V)
            vm = V.tag(nc.vector.tensor_max(sc1[:, :], sc1[:, :], sc2[:, :]))
        ps_done(4, V, vm)
        ps_done(5, V, vm)
    bank = 6
    ps_wait(bank)
    T_.wait(V)
    T_.tag(nc.tensor.transpose(ps[bank][0:1, 0:128], sc1[:, 0:1], ident[:, :]))
    V.wait(T_)
    vm = V.tag(nc.vector.reduce_max(gsc[0:1, 0:1], ps[bank][0:1, 0:128],
                                    axis=AX.X))
    ps_done(bank, V, vm)
    LS.wait(V, vm)
    ccm = load(ccin_h[0][:, :], gsc[0:1, 0:1])
    G.wait(LS, ccm)
    cc1 = CC.tag(nc.gpsimd.collective_compute(
        "AllReduce", ALU.max, ins=[ccin_h[0][:, :]], outs=[ccout_h[0][:, :]],
        replica_groups=[list(range(8))]))
    LS.wait(CC, cc1)
    gmk = load(gsc[0:1, 2:3], ccout_h[0][:, :])
    S.wait(LS, gmk)
    sm = S.tag(nc.scalar.activation(gsc[0:1, 3:4], clip_sb[0:1, 0:1],
                                    AF.Sigmoid))
    V.wait(S, sm)
    V.tag(nc.vector.tensor_mul(gsc[0:1, 3:4], gsc[0:1, 3:4], gsc[0:1, 2:3]))
    V.wait(V)
    vm = V.tag(nc.vector.tensor_scalar_mul(gsc[0:1, 6:7], gsc[0:1, 3:4],
                                           1.0 / 127.0))
    S.wait(V, vm)
    S.tag(nc.scalar.activation(gsc[0:1, 4:5], gsc[0:1, 3:4], AF.Ln))
    S.wait(S)
    sm = S.tag(nc.scalar.activation(gsc[0:1, 5:6], gsc[0:1, 4:5], AF.Exp,
                                    bias=consts[0:1, C_LN127:C_LN127 + 1],
                                    scale=consts[0:1, C_NEG1:C_NEG1 + 1]))
    bank = 7
    ps_wait(bank)
    T_.wait(S, sm)
    T_.wait(V, vm)
    T_.tag(MM(ps[bank][:, 0:1], ones1r[:, :], gsc[0:1, 5:6], start=True,
              stop=True))
    V.wait(T_)
    cm = V.tag(nc.vector.tensor_copy(sc3[:, :], ps[bank][:, 0:1]))
    ps_done(bank, V, cm)
    bank = 6
    ps_wait(bank)
    T_.tag(MM(ps[bank][:, 0:1], ones1r[:, :], gsc[0:1, 6:7], start=True,
              stop=True))
    V.wait(T_)
    cm = V.tag(nc.vector.tensor_copy(sc4[:, :], ps[bank][:, 0:1]))
    ps_done(bank, V, cm)
    V.wait(V)
    for dc in range(NCH):
        V.tag(nc.vector.tensor_scalar(cqcol[:, dc:dc + 1],
                                      modv[(0, "cb1")][:, dc:dc + 1],
                                      sc3[:, 0:1], None, op0=ALU.mult))
        V.tag(nc.vector.tensor_scalar(sqcol[:, dc:dc + 1],
                                      modv[(0, "sb")][:, dc:dc + 1],
                                      sc3[:, 0:1], None, op0=ALU.mult))
    V.tag(nc.vector.tensor_scalar(swg_s[:, :], swg[:, :], sc4[:, 0:1], None,
                                  op0=ALU.mult))
    vm = V.tag(nc.vector.tensor_scalar(swu_s[:, :], swu[:, :], sc4[:, 0:1],
                                       None, op0=ALU.mult))
    for t in range(NTC):
        ln_prep_t(x_sb, t)
        vm = 0
        for dc in range(NCH):
            vm = ln_u(tscr2[:, 0:512], x_sb, dc, t)
            S.wait(V, vm)
            sm = S.tag(nc.scalar.activation(tscr[:, 0:512], tscr2[:, 0:512],
                                            AF.Identity,
                                            bias=sqcol[:, dc:dc + 1],
                                            scale=cqcol[:, dc:dc + 1]))
            V.wait(S, sm)
            V.tag(nc.vector.tensor_scalar(tscr[:, 0:512], tscr[:, 0:512],
                                          -128.0, 127.0, op0=ALU.max,
                                          op1=ALU.min))
            vm = V.tag(nc.vector.tensor_scalar(
                hb[:, dc * T + t * 512: dc * T + (t + 1) * 512],
                tscr[:, 0:512], MAGIC, MAGIC, op0=ALU.add, op1=ALU.subtract))
        ps_done(4, V, vm)
        ps_done(5, V, vm)
    iq_ready = V.cnt
    dbg("iq0", hb[:, :])
    dbg("gsc4", gsc[:, :])
    dbg("swg", swg[:, :])
    dbg("swgs", swg_s[:, :])

    V.tag(nc.vector.memset(sc1[:, :], 0.0))
    pstore = 0
    for t in range(NTC):
        for ic in range(NI0):
            bg = ic % 2
            bu = 2 + ic % 2
            ps_wait(bg)
            ps_wait(bu)
            for dc in range(NCH):
                mg = load(wstr_a[:, 0:128],
                          wgT_h[dc * 128:(dc + 1) * 128,
                                ic * 128:(ic + 1) * 128], q=LA)
                mu = load(wstr_b[:, 0:128],
                          wuT_h[dc * 128:(dc + 1) * 128,
                                ic * 128:(ic + 1) * 128], q=LA)
                T_.wait(LA, mu)
                T_.wait(V, iq_ready)
                T_.tag(MM(ps[bg][:, 0:512], wstr_a[:, 0:128],
                          hb[:, dc * T + t * 512: dc * T + (t + 1) * 512],
                          start=(dc == 0), stop=(dc == NCH - 1)))
                T_.tag(MM(ps[bu][:, 0:512], wstr_b[:, 0:128],
                          hb[:, dc * T + t * 512: dc * T + (t + 1) * 512],
                          start=(dc == 0), stop=(dc == NCH - 1)))
                barrier_LA()
            S.wait(T_)
            sm = S.tag(nc.scalar.activation(tscr[:, 0:512], ps[bg][:, 0:512],
                                            AF.Silu, bias=0.0,
                                            scale=swg_s[:, ic:ic + 1]))
            ps_done(bg, S, sm)
            V.wait(S, sm)
            V.tag(nc.vector.tensor_scalar(tscr[:, 512:1024], ps[bu][:, 0:512],
                                          swu_s[:, ic:ic + 1], None,
                                          op0=ALU.mult))
            pb_ = (ic % 2) * 512
            V.wait(LG, pstore)
            vm = V.tag(nc.vector.tensor_mul(prodt_f[:, pb_:pb_ + 512],
                                            tscr[:, 0:512], tscr[:, 512:1024]))
            ps_done(bu, V, vm)
            V.tag(nc.vector.reduce_max(sc2[:, :], prodt_f[:, pb_:pb_ + 512],
                                       axis=AX.X, apply_absolute_value=True))
            V.wait(V)
            V.tag(nc.vector.tensor_max(sc1[:, :], sc1[:, :], sc2[:, :]))
            G.wait(V)
            pstore = store(prod_h[ic * 128:(ic + 1) * 128,
                                  t * 512:(t + 1) * 512],
                           prodt_f[:, pb_:pb_ + 512])
    bank = 6
    ps_wait(bank)
    T_.wait(V)
    T_.tag(nc.tensor.transpose(ps[bank][0:1, 0:128], sc1[:, 0:1], ident[:, :]))
    V.wait(T_)
    vm = V.tag(nc.vector.reduce_max(gsc[0:1, 0:1], ps[bank][0:1, 0:128],
                                    axis=AX.X))
    ps_done(bank, V, vm)
    LS.wait(V, vm)
    ccm = load(ccin_h[1][:, :], gsc[0:1, 0:1])
    G.wait(LS, ccm)
    G.wait(LG)
    cc2 = CC.tag(nc.gpsimd.collective_compute(
        "AllReduce", ALU.max, ins=[ccin_h[1][:, :]], outs=[ccout_h[1][:, :]],
        replica_groups=[list(range(8))]))
    LS.wait(CC, cc2)
    gmk = load(gsc[0:1, 2:3], ccout_h[1][:, :])
    S.wait(LS, gmk)
    sm = S.tag(nc.scalar.activation(gsc[0:1, 3:4], clip_sb[0:1, 1:2],
                                    AF.Sigmoid))
    V.wait(S, sm)
    V.tag(nc.vector.tensor_mul(gsc[0:1, 3:4], gsc[0:1, 3:4], gsc[0:1, 2:3]))
    V.wait(V)
    vm = V.tag(nc.vector.tensor_scalar_mul(gsc[0:1, 6:7], gsc[0:1, 3:4],
                                           1.0 / 127.0))
    S.wait(V, vm)
    S.tag(nc.scalar.activation(gsc[0:1, 4:5], gsc[0:1, 3:4], AF.Ln))
    S.wait(S)
    sm = S.tag(nc.scalar.activation(gsc[0:1, 5:6], gsc[0:1, 4:5], AF.Exp,
                                    bias=consts[0:1, C_LN127:C_LN127 + 1],
                                    scale=consts[0:1, C_NEG1:C_NEG1 + 1]))
    bank = 7
    ps_wait(bank)
    T_.wait(S, sm)
    T_.wait(V, vm)
    T_.tag(MM(ps[bank][:, 0:1], ones1r[:, :], gsc[0:1, 5:6], start=True,
              stop=True))
    V.wait(T_)
    cm = V.tag(nc.vector.tensor_copy(sc3[:, :], ps[bank][:, 0:1]))
    ps_done(bank, V, cm)
    bank = 6
    ps_wait(bank)
    T_.tag(MM(ps[bank][:, 0:1], ones1r[:, :], gsc[0:1, 6:7], start=True,
              stop=True))
    V.wait(T_)
    cm = V.tag(nc.vector.tensor_copy(sc4[:, :], ps[bank][:, 0:1]))
    ps_done(bank, V, cm)
    V.wait(V)
    for dc in range(NCH):
        V.tag(nc.vector.tensor_scalar(swd_s[:, dc:dc + 1], swd[:, dc:dc + 1],
                                      sc4[:, 0:1], None, op0=ALU.mult))
        V.wait(V)
        V.tag(nc.vector.tensor_mul(swd_s[:, dc:dc + 1], swd_s[:, dc:dc + 1],
                                   modv[(0, "gb")][:, dc:dc + 1]))
        V.wait(V)
    rq2_ready = V.cnt

    for t in range(NTC):
        for ic in range(NI0):
            pb_ = (ic % 2) * 512
            mk = load(prodt_f[:, pb_:pb_ + 512],
                      prod_h[ic * 128:(ic + 1) * 128, t * 512:(t + 1) * 512])
            V.wait(LS, mk)
            V.wait(V, rq2_ready)
            V.tag(nc.vector.tensor_scalar(tscr[:, 0:512],
                                          prodt_f[:, pb_:pb_ + 512],
                                          sc3[:, 0:1], None, op0=ALU.mult))
            V.tag(nc.vector.tensor_scalar(tscr[:, 0:512], tscr[:, 0:512],
                                          -128.0, 127.0, op0=ALU.max,
                                          op1=ALU.min))
            vm = V.tag(nc.vector.tensor_scalar(
                iq2t_bf[:, ic * 512:(ic + 1) * 512], tscr[:, 0:512], MAGIC,
                MAGIC, op0=ALU.add, op1=ALU.subtract))
            LS.wait(V, vm)
        iq2_ready = V.cnt
        for hc in range(NCH):
            bank = hc % 2
            ps_wait(bank)
            for ic in range(NI0):
                rb = wstr_a if ic % 2 == 0 else wstr_b
                mw = load(rb[:, 0:128],
                          wdT_h[ic * 128:(ic + 1) * 128,
                                hc * 128:(hc + 1) * 128], q=LA)
                T_.wait(LA, mw)
                T_.wait(V, iq2_ready)
                T_.tag(MM(ps[bank][:, 0:512], rb[:, 0:128],
                          iq2t_bf[:, ic * 512:(ic + 1) * 512],
                          start=(ic == 0), stop=(ic == NI0 - 1)))
                barrier_LA()
            V.wait(T_)
            cm = V.tag(nc.vector.scalar_tensor_tensor(
                x_sb[:, hc * T + t * 512: hc * T + (t + 1) * 512],
                ps[bank][:, 0:512], swd_s[:, hc:hc + 1],
                x_sb[:, hc * T + t * 512: hc * T + (t + 1) * 512],
                op0=ALU.mult, op1=ALU.add))
            ps_done(bank, V, cm)
    x_ffn0 = V.cnt
    dbg("xf0", x_sb[:, :])

    prep_ternary(w1_h, I1, D, wrow_h, w1T_h, 0)
    prep_ternary(w2_h, D, I1, wrow2_h, w2T_h, 1)

    V.wait(V, x_ffn0)
    barrier_CMP()
    for t in range(NTC):
        ln_prep_t(x_sb, t)
        vm = 0
        for dc in range(NCH):
            vm = ln_u(tscr2[:, 0:512], x_sb, dc, t)
            S.wait(V, vm)
            sm = S.tag(nc.scalar.activation(
                hb[:, dc * T + t * 512: dc * T + (t + 1) * 512],
                tscr2[:, 0:512], AF.Identity,
                bias=modv[(1, "sa")][:, dc:dc + 1],
                scale=modv[(1, "ca1")][:, dc:dc + 1]))
            V.wait(S, sm)
        ps_done(4, V, vm)
        ps_done(5, V, vm)
    h_ready = S.cnt
    G.wait(S, h_ready)
    G.wait(V)
    for dc in range(NCH):
        store(xspill_h[:, dc * T:(dc + 1) * T], x_sb[:, dc * T:(dc + 1) * T])
    xsp_done = LG.cnt
    LS.wait(LG, xsp_done)
    load(cs64[:, :], cs64_h[:, :])
    rope_ready = load(sn64[:, :], sn64_h[:, :])

    barrier_LA()
    for ct in range(NTT):
        for nf in range(2):
            bank = nf
            ps_wait(bank)
            for dc in range(NCH):
                mk = load(tscr2[:, 1024:1536],
                          wv1_h[dc * 128:(dc + 1) * 128,
                                nf * 512:(nf + 1) * 512], q=LA)
                V.wait(LA, mk)
                V.wait(T_)
                vm = V.tag(nc.vector.tensor_copy(wstr_a[:, 0:512],
                                                 tscr2[:, 1024:1536]))
                T_.wait(V, vm)
                T_.wait(S, h_ready)
                T_.tag(MM(ps[bank][:, 0:512],
                          hb[:, dc * T + ct * 128: dc * T + (ct + 1) * 128],
                          wstr_a[:, 0:512], start=(dc == 0),
                          stop=(dc == NCH - 1)))
                barrier_LA()
                LA.wait(V, vm)
            V.wait(T_)
            vm = V.tag(nc.vector.tensor_copy(
                v_sb[:, ct * 1040 + nf * 512: ct * 1040 + nf * 512 + 512],
                ps[bank][:, 0:512]))
            ps_done(bank, V, vm)
    for c in range(8):
        bank = 6
        ps_wait(bank)
        T_.wait(LS, m_base)
        T_.wait(V)
        T_.tag(nc.tensor.transpose(ps[bank][0:1, 0:128], bv[:, c:c + 1],
                                   ident[:, :]))
        V.wait(T_)
        cm = V.tag(nc.vector.tensor_copy(
            tscr[0:1, 1024 + c * 128: 1024 + (c + 1) * 128],
            ps[bank][0:1, 0:128]))
        ps_done(bank, V, cm)
    for nf in range(2):
        bank = 6 + nf
        ps_wait(bank)
        T_.wait(V)
        T_.tag(MM(ps[bank][:, 0:512], ones1r[:, :],
                  tscr[0:1, 1024 + nf * 512: 1024 + (nf + 1) * 512],
                  start=True, stop=True))
    V.wait(T_)
    vm = 0
    for ct in range(NTT):
        for nf in range(2):
            vm = V.tag(nc.vector.tensor_add(
                v_sb[:, ct * 1040 + nf * 512: ct * 1040 + nf * 512 + 512],
                v_sb[:, ct * 1040 + nf * 512: ct * 1040 + nf * 512 + 512],
                ps[6 + nf][:, 0:512]))
    ps_done(6, V, vm)
    ps_done(7, V, vm)
    v_ready = V.cnt

    for h in range(H):
        pc_ = h // 2
        par = h % 2
        for (jq, dst) in ((0, qh_bf), (2, kh_bf)):
            for t in range(NTC):
                bA = 2 + (t % 2)
                bB = 0 + (t % 2)
                ps_wait(bA)
                ps_wait(bB)
                for dc in range(NCH):
                    mk = load(tscr2[:, 1024:1152],
                              wqkvr_h[dc * 128:(dc + 1) * 128,
                                      h * 256 + jq * 64:
                                      h * 256 + jq * 64 + 128], q=LA)
                    V.wait(LA, mk)
                    V.wait(T_)
                    vm = V.tag(nc.vector.tensor_copy(wstr_a[:, 512:640],
                                                     tscr2[:, 1024:1152]))
                    T_.wait(V, vm)
                    T_.tag(MM(ps[bA][0:64, 0:512], wstr_a[:, 512:576],
                              hb[:, dc * T + t * 512: dc * T + (t + 1) * 512],
                              start=(dc == 0), stop=(dc == NCH - 1)))
                    T_.tag(MM(ps[bB][0:64, 0:512], wstr_a[:, 576:640],
                              hb[:, dc * T + t * 512: dc * T + (t + 1) * 512],
                              start=(dc == 0), stop=(dc == NCH - 1)))
                    barrier_LA()
                    LA.wait(V, vm)
                S.wait(T_)
                S.wait(V)
                sm = S.tag(nc.scalar.activation(
                    rope_f[:, t * 512:(t + 1) * 512], ps[bA][0:64, 0:512],
                    AF.Identity, bias=biasr[:, h * 4 + jq: h * 4 + jq + 1],
                    scale=1.0))
                ps_done(bA, S, sm)
                sm = S.tag(nc.scalar.activation(
                    rope_g[:, t * 512:(t + 1) * 512], ps[bB][0:64, 0:512],
                    AF.Identity, bias=biasr[:, h * 4 + jq + 1: h * 4 + jq + 2],
                    scale=1.0))
                ps_done(bB, S, sm)
                sm = S.tag(nc.scalar.activation(
                    nsq6[:, t * 512:(t + 1) * 512],
                    rope_f[:, t * 512:(t + 1) * 512], AF.Square))
            V.wait(S)
            V.wait(LS, rope_ready)
            for t in range(NTC):
                sl_ = slice(t * 512, (t + 1) * 512)
                V.tag(nc.vector.tensor_mul(tscr[0:64, 0:512], rope_f[:, sl_],
                                           cs64[:, sl_]))
                V.tag(nc.vector.tensor_mul(tscr[0:64, 512:1024],
                                           rope_g[:, sl_], sn64[:, sl_]))
                V.tag(nc.vector.tensor_add(dst[:, sl_], tscr[0:64, 0:512],
                                           tscr[0:64, 512:1024]))
            T_.wait(S)
            V.tag(nc.vector.memset(sc1[:, :], 0.0))
            for ct in range(NTT):
                bank = 6 + (ct % 2)
                ps_wait(bank)
                T_.tag(MM(ps[bank][:, 0:1], nsq6[:, ct * 128:(ct + 1) * 128],
                          onesc[0:64, :], start=True, stop=True))
                V.wait(T_)
                V.wait(V)
                cm = V.tag(nc.vector.tensor_max(sc1[:, :], sc1[:, :],
                                                ps[bank][:, 0:1]))
                ps_done(bank, V, cm)
            bank = 6
            ps_wait(bank)
            T_.wait(V)
            T_.tag(nc.tensor.transpose(ps[bank][0:1, 0:128], sc1[:, 0:1],
                                       ident[:, :]))
            V.wait(T_)
            slot = 4 if jq == 0 else 5
            vm = V.tag(nc.vector.reduce_max(gsc[0:1, slot:slot + 1],
                                            ps[bank][0:1, 0:128], axis=AX.X))
            ps_done(bank, V, vm)
        kq_ready = V.cnt
        V.wait(V)
        V.tag(nc.vector.tensor_mul(gsc[0:1, 0:1], gsc[0:1, 4:5],
                                   gsc[0:1, 5:6]))
        vm = V.cnt
        S.wait(V, vm)
        sm = S.tag(nc.scalar.activation(gsc[0:1, 1:2], gsc[0:1, 0:1], AF.Sqrt))
        bank = 7
        ps_wait(bank)
        T_.wait(S, sm)
        T_.tag(MM(ps[bank][:, 0:1], ones1r[:, :], gsc[0:1, 1:2], start=True,
                  stop=True))
        V.wait(T_)
        vm = V.tag(nc.vector.tensor_scalar_mul(sc2[:, :], ps[bank][:, 0:1],
                                               -0.125 * 1.0001))
        ps_done(bank, V, vm)
        S.wait(V, vm)

        for t in range(NTC):
            for st_ in range(NTT):
                bank = st_ % 2
                ps_wait(bank)
                T_.wait(V, kq_ready)
                T_.tag(MM(ps[bank][0:128, 0:512],
                          kh_bf[:, st_ * 128:(st_ + 1) * 128],
                          qh_bf[:, t * 512:(t + 1) * 512], start=True,
                          stop=True))
                S.wait(T_)
                sm = S.tag(nc.scalar.activation(
                    sl1_bf[:, st_ * 512:(st_ + 1) * 512], ps[bank][:, 0:512],
                    AF.Exp, bias=sc2[:, 0:1],
                    scale=consts[:, C_SC0125:C_SC0125 + 1]))
                ps_done(bank, S, sm)
            bank = 2
            bank2 = 3
            ps_wait(bank)
            ps_wait(bank2)
            T_.wait(S)
            tp = None if par == 0 else (0, 64)
            rows = slice(0, 64) if par == 0 else slice(64, 128)
            for st_ in range(NTT):
                T_.tag(MM(ps[bank][rows, 0:512],
                          v_sb[:, st_ * 1040 + h * 64: st_ * 1040 + h * 64 + 64],
                          sl1_bf[:, st_ * 512:(st_ + 1) * 512],
                          start=(st_ == 0), stop=(st_ == NTT - 1),
                          tile_position=tp))
            for st_ in range(NTT):
                T_.tag(MM(ps[bank2][0:1, 0:512], onesc_b[:, 0:1],
                          sl1_bf[:, st_ * 512:(st_ + 1) * 512],
                          start=(st_ == 0), stop=(st_ == NTT - 1)))
            V.wait(T_)
            vv2 = V.tag(nc.vector.tensor_scalar(tscr2[0:1, 1536:2048],
                                                ps[bank2][0:1, 0:512],
                                                2.0 ** -96, 2.0 ** 26,
                                                op0=ALU.max, op1=ALU.mult))
            ps_done(bank2, V, vv2)
            S.wait(V, vv2)
            sm = S.tag(nc.scalar.activation(tscr2[0:1, 1536:2048],
                                            tscr2[0:1, 1536:2048], AF.Ln))
            S.wait(S)
            sm = S.tag(nc.scalar.activation(tscr2[0:1, 1536:2048],
                                            tscr2[0:1, 1536:2048], AF.Exp,
                                            bias=consts[0:1, C_LN2_26:C_LN2_26 + 1],
                                            scale=consts[0:1, C_NEG1:C_NEG1 + 1]))
            bank3 = 6
            ps_wait(bank3)
            T_.wait(S, sm)
            T_.tag(MM(ps[bank3][:, 0:512], ones1r[:, :], tscr2[0:1, 1536:2048],
                      start=True, stop=True))
            V.wait(T_)
            vm = V.tag(nc.vector.tensor_copy(s1b[:, 0:512],
                                             ps[bank3][:, 0:512]))
            ps_done(bank3, V, vm)
            cm = V.tag(nc.vector.tensor_mul(
                z_sb[rows, pc_ * T + t * 512: pc_ * T + (t + 1) * 512],
                ps[bank][rows, 0:512], s1b[rows, 0:512]))
            ps_done(bank, V, cm)
    z_ready = V.cnt

    for e in range(NCH):
        V.tag(nc.vector.tensor_mul(gab[:, e:e + 1], modv[(1, "ga")][:, e:e + 1],
                                   bout[:, e:e + 1]))
    LS.wait(T_)
    LS.wait(V)
    LS.wait(S)
    barrier_LA()
    for e in range(NCH):
        mk = load(x_sb[:, e * T:(e + 1) * T], xspill_h[:, e * T:(e + 1) * T])
        for t in range(NTC):
            bank = t % 2
            ps_wait(bank)
            for fc in range(NCH):
                mw = load(tscr2[:, 1024:1152],
                          wout_h[fc * 128:(fc + 1) * 128,
                                 e * 128:(e + 1) * 128], q=LA)
                V.wait(LA, mw)
                V.wait(T_)
                vm = V.tag(nc.vector.tensor_copy(wbuf[:, 0:128],
                                                 tscr2[:, 1024:1152]))
                T_.wait(V, vm)
                T_.wait(V, z_ready)
                T_.tag(MM(ps[bank][:, 0:512], wbuf[:, 0:128],
                          z_sb[:, fc * T + t * 512: fc * T + (t + 1) * 512],
                          start=(fc == 0), stop=(fc == NCH - 1)))
                barrier_LA()
                LA.wait(V, vm)
            V.wait(T_)
            V.wait(LS, mk)
            V.tag(nc.vector.scalar_tensor_tensor(
                x_sb[:, e * T + t * 512: e * T + (t + 1) * 512],
                ps[bank][:, 0:512], modv[(1, "ga")][:, e:e + 1],
                x_sb[:, e * T + t * 512: e * T + (t + 1) * 512],
                op0=ALU.mult, op1=ALU.add))
            cm = V.tag(nc.vector.tensor_scalar(
                x_sb[:, e * T + t * 512: e * T + (t + 1) * 512],
                x_sb[:, e * T + t * 512: e * T + (t + 1) * 512],
                gab[:, e:e + 1], None, op0=ALU.add))
            ps_done(bank, V, cm)
    x_attn1 = V.cnt
    dbg("xa1", x_sb[:, :])
    dbg("z1", z_sb[:, :])

    V.wait(V, x_attn1)
    barrier_CMP()
    V.tag(nc.vector.memset(tscr2[:, 0:2048], 0.0))
    for t in range(NTC):
        ln_prep_t(x_sb, t)
        vm = 0
        for dc in range(NCH):
            vm = ln_u(tscr[:, 0:512], x_sb, dc, t)
            S.wait(V, vm)
            sm = S.tag(nc.scalar.activation(tscr[:, 1024:1536], tscr[:, 0:512],
                                            AF.Identity,
                                            bias=modv[(1, "sb")][:, dc:dc + 1],
                                            scale=modv[(1, "cb1")][:, dc:dc + 1]))
            S.wait(S, sm)
            sm = S.tag(nc.scalar.activation(tscr[:, 1536:2048],
                                            tscr[:, 1024:1536], AF.Abs))
            V.wait(S, sm)
            vm = V.tag(nc.vector.tensor_max(tscr2[:, t * 512:(t + 1) * 512],
                                            tscr2[:, t * 512:(t + 1) * 512],
                                            tscr[:, 1536:2048]))
        ps_done(4, V, vm)
        ps_done(5, V, vm)
    T_.wait(V)
    for tt in range(NTT):
        bank = 6 + (tt % 2)
        ps_wait(bank)
        T_.tag(nc.tensor.transpose(ps[bank][:, 0:128],
                                   tscr2[:, tt * 128:(tt + 1) * 128],
                                   ident[:, :]))
        V.wait(T_)
        cm = V.tag(nc.vector.reduce_max(tmax[:, tt:tt + 1],
                                        ps[bank][:, 0:128], axis=AX.X))
        ps_done(bank, V, cm)
    V.wait(V)
    vm = V.tag(nc.vector.tensor_single_scalar(kn2c[:, :], tmax[:, :], 1e-5,
                                              op=ALU.max))
    S.wait(V, vm)
    S.tag(nc.scalar.activation(racc[:, :], kn2c[:, :], AF.Ln))
    S.wait(S)
    sm = S.tag(nc.scalar.activation(racc[:, :], racc[:, :], AF.Exp,
                                    bias=consts[:, C_LN127:C_LN127 + 1],
                                    scale=consts[:, C_NEG1:C_NEG1 + 1]))
    V.wait(S, sm)
    vm = V.tag(nc.vector.tensor_scalar_mul(rinv[:, :], kn2c[:, :],
                                           1.0 / 127.0))
    T_.wait(V, vm)
    for tt in range(NTT):
        bank = 6 + (tt % 2)
        ps_wait(bank)
        T_.tag(nc.tensor.transpose(ps[bank][0:1, 0:128], racc[:, tt:tt + 1],
                                   ident[:, :]))
        V.wait(T_)
        cm = V.tag(nc.vector.tensor_copy(r1row[0:1, tt * 128:(tt + 1) * 128],
                                         ps[bank][0:1, 0:128]))
        ps_done(bank, V, cm)
    for tt in range(NTT):
        bank = 6 + (tt % 2)
        ps_wait(bank)
        T_.tag(nc.tensor.transpose(ps[bank][0:1, 0:128], rinv[:, tt:tt + 1],
                                   ident[:, :]))
        V.wait(T_)
        cm = V.tag(nc.vector.tensor_copy(tscr2[0:1, tt * 128:(tt + 1) * 128],
                                         ps[bank][0:1, 0:128]))
        ps_done(bank, V, cm)
    vm = V.tag(nc.vector.tensor_scalar(tscr2[0:1, 0:2048], tscr2[0:1, 0:2048],
                                       swsc[0:1, 0:1], None, op0=ALU.mult))
    s1_ready = vm
    r1_ready = vm

    for t in range(NTC):
        bank_r = 3
        ps_wait(bank_r)
        T_.wait(V, r1_ready)
        for q_ in range(4):
            T_.tag(MM(ps[bank_r][:, q_ * 128:(q_ + 1) * 128], ones1r[:, :],
                      r1row[0:1, (t * 4 + q_) * 128: (t * 4 + q_ + 1) * 128],
                      start=(q_ == 0), stop=(q_ == 3)))
        ln_prep_t(x_sb, t)
        vm = 0
        for dc in range(NCH):
            vm = ln_u(tscr[:, 0:512], x_sb, dc, t)
            S.wait(V, vm)
            sm = S.tag(nc.scalar.activation(tscr[:, 1024:1536], tscr[:, 0:512],
                                            AF.Identity,
                                            bias=modv[(1, "sb")][:, dc:dc + 1],
                                            scale=modv[(1, "cb1")][:, dc:dc + 1]))
            V.wait(S, sm)
            V.tag(nc.vector.tensor_mul(tscr[:, 1536:2048], tscr[:, 1024:1536],
                                       ps[bank_r][:, 0:512]))
            V.tag(nc.vector.tensor_scalar(tscr[:, 1536:2048],
                                          tscr[:, 1536:2048], -128.0, 127.0,
                                          op0=ALU.max, op1=ALU.min))
            vm = V.tag(nc.vector.tensor_scalar(
                hb[:, dc * T + t * 512: dc * T + (t + 1) * 512],
                tscr[:, 1536:2048], MAGIC, MAGIC, op0=ALU.add,
                op1=ALU.subtract))
        ps_done(4, V, vm)
        ps_done(5, V, vm)
        ps_done(bank_r, V, vm)
    iq1_ready = V.cnt

    hst = 0
    for t in range(NTC):
        bank_r = 3
        ps_wait(bank_r)
        T_.wait(V, s1_ready)
        for q_ in range(4):
            T_.tag(MM(ps[bank_r][:, q_ * 128:(q_ + 1) * 128], ones1r[:, :],
                      tscr2[0:1, (t * 4 + q_) * 128: (t * 4 + q_ + 1) * 128],
                      start=(q_ == 0), stop=(q_ == 3)))
        V.wait(T_)
        vm = V.tag(nc.vector.tensor_copy(s1b[:, 0:512], ps[bank_r][:, 0:512]))
        ps_done(bank_r, V, vm)
        for ic in range(NI1):
            bank = ic % 2
            ps_wait(bank)
            for dc in range(NCH):
                rb = wstr_a if dc % 2 == 0 else wstr_b
                mw = load(rb[:, 0:128],
                          w1T_h[dc * 128:(dc + 1) * 128,
                                ic * 128:(ic + 1) * 128], q=LA)
                T_.wait(LA, mw)
                T_.wait(V, iq1_ready)
                T_.tag(MM(ps[bank][:, 0:512], rb[:, 0:128],
                          hb[:, dc * T + t * 512: dc * T + (t + 1) * 512],
                          start=(dc == 0), stop=(dc == NCH - 1)))
                barrier_LA()
            V.wait(T_)
            V.wait(LG, hst)
            vm = V.tag(nc.vector.tensor_mul(tscr[:, 0:512], ps[bank][:, 0:512],
                                            s1b[:, 0:512]))
            ps_done(bank, V, vm)
            G.wait(V, vm)
            hst = store(h1_h[ic * 128:(ic + 1) * 128, t * 512:(t + 1) * 512],
                        tscr[:, 0:512])
    h1_done = LG.cnt

    LS.wait(LG, h1_done)
    V.tag(nc.vector.memset(tscr2[:, 0:2048], 0.0))
    for t in range(NTC):
        s0 = 1 if t == 0 else 0
        lo_c = max(t * 512 - 1, 0)
        w_c = 514 - s0 - (1 if t == NTC - 1 else 0)
        for ic in range(NI1):
            mk = load(tscr[:, s0:s0 + w_c],
                      h1_h[ic * 128:(ic + 1) * 128, lo_c:lo_c + w_c])
            V.wait(LS, mk)
            if t == 0:
                V.tag(nc.vector.memset(tscr[:, 0:1], 0.0))
            if t == NTC - 1:
                V.tag(nc.vector.memset(tscr[:, 513:514], 0.0))
            V.tag(nc.vector.tensor_scalar(tscr[:, 514:1026], tscr[:, 0:512],
                                          convw[:, ic * 3 + 0: ic * 3 + 1],
                                          None, op0=ALU.mult))
            V.tag(nc.vector.scalar_tensor_tensor(
                tscr[:, 514:1026], tscr[:, 1:513],
                convw[:, ic * 3 + 1: ic * 3 + 2], tscr[:, 514:1026],
                op0=ALU.mult, op1=ALU.add))
            vm = V.tag(nc.vector.scalar_tensor_tensor(
                tscr[:, 514:1026], tscr[:, 2:514],
                convw[:, ic * 3 + 2: ic * 3 + 3], tscr[:, 514:1026],
                op0=ALU.mult, op1=ALU.add))
            S.wait(V, vm)
            sm = S.tag(nc.scalar.activation(tscr[:, 1026:1538],
                                            tscr[:, 514:1026], AF.Silu,
                                            bias=convb[:, ic:ic + 1],
                                            scale=1.0))
            S.wait(S)
            sm2 = S.tag(nc.scalar.activation(s1b[:, 0:512], tscr[:, 1026:1538],
                                             AF.Abs))
            V.wait(S, sm2)
            vm = V.tag(nc.vector.tensor_max(tscr2[:, t * 512:(t + 1) * 512],
                                            tscr2[:, t * 512:(t + 1) * 512],
                                            s1b[:, 0:512]))
            G.wait(S, sm)
            G.wait(V, vm)
            store(y_h[ic * 128:(ic + 1) * 128, t * 512:(t + 1) * 512],
                  tscr[:, 1026:1538])
            LS.wait(V, vm)
            LS.wait(LG)
            S.wait(LG)
    y_done = LG.cnt
    T_.wait(V)
    for tt in range(NTT):
        bank = 6 + (tt % 2)
        ps_wait(bank)
        T_.tag(nc.tensor.transpose(ps[bank][:, 0:128],
                                   tscr2[:, tt * 128:(tt + 1) * 128],
                                   ident[:, :]))
        V.wait(T_)
        cm = V.tag(nc.vector.reduce_max(tmax[:, tt:tt + 1],
                                        ps[bank][:, 0:128], axis=AX.X))
        ps_done(bank, V, cm)
    V.wait(V)
    vm = V.tag(nc.vector.tensor_single_scalar(kn2c[:, :], tmax[:, :], 1e-5,
                                              op=ALU.max))
    S.wait(V, vm)
    S.tag(nc.scalar.activation(racc[:, :], kn2c[:, :], AF.Ln))
    S.wait(S)
    sm = S.tag(nc.scalar.activation(racc[:, :], racc[:, :], AF.Exp,
                                    bias=consts[:, C_LN127:C_LN127 + 1],
                                    scale=consts[:, C_NEG1:C_NEG1 + 1]))
    V.wait(S, sm)
    vm = V.tag(nc.vector.tensor_scalar_mul(rinv[:, :], kn2c[:, :],
                                           1.0 / 127.0))
    T_.wait(V, vm)
    for tt in range(NTT):
        bank = 6 + (tt % 2)
        ps_wait(bank)
        T_.tag(nc.tensor.transpose(ps[bank][0:1, 0:128], racc[:, tt:tt + 1],
                                   ident[:, :]))
        V.wait(T_)
        cm = V.tag(nc.vector.tensor_copy(r1row[0:1, tt * 128:(tt + 1) * 128],
                                         ps[bank][0:1, 0:128]))
        ps_done(bank, V, cm)
    for tt in range(NTT):
        bank = 6 + (tt % 2)
        ps_wait(bank)
        T_.tag(nc.tensor.transpose(ps[bank][0:1, 0:128], rinv[:, tt:tt + 1],
                                   ident[:, :]))
        V.wait(T_)
        cm = V.tag(nc.vector.tensor_copy(tscr2[0:1, tt * 128:(tt + 1) * 128],
                                         ps[bank][0:1, 0:128]))
        ps_done(bank, V, cm)
    vm = V.tag(nc.vector.tensor_scalar(tscr2[0:1, 0:2048], tscr2[0:1, 0:2048],
                                       swsc[0:1, 1:2], None, op0=ALU.mult))
    r2_ready = vm

    LS.wait(LG, y_done)
    for t in range(NTC):
        bank_r = 3
        ps_wait(bank_r)
        T_.wait(V, r2_ready)
        for q_ in range(4):
            T_.tag(MM(ps[bank_r][:, q_ * 128:(q_ + 1) * 128], ones1r[:, :],
                      r1row[0:1, (t * 4 + q_) * 128: (t * 4 + q_ + 1) * 128],
                      start=(q_ == 0), stop=(q_ == 3)))
        bank_s = 2
        ps_wait(bank_s)
        for q_ in range(4):
            T_.tag(MM(ps[bank_s][:, q_ * 128:(q_ + 1) * 128], ones1r[:, :],
                      tscr2[0:1, (t * 4 + q_) * 128: (t * 4 + q_ + 1) * 128],
                      start=(q_ == 0), stop=(q_ == 3)))
        V.wait(T_)
        vm = V.tag(nc.vector.tensor_copy(s1b[:, 0:512], ps[bank_s][:, 0:512]))
        ps_done(bank_s, V, vm)
        for ic in range(NI1):
            mk = load(tscr[:, 0:512],
                      y_h[ic * 128:(ic + 1) * 128, t * 512:(t + 1) * 512])
            V.wait(LS, mk)
            V.tag(nc.vector.tensor_mul(tscr[:, 512:1024], tscr[:, 0:512],
                                       ps[bank_r][:, 0:512]))
            V.tag(nc.vector.tensor_scalar(tscr[:, 512:1024],
                                          tscr[:, 512:1024], -128.0, 127.0,
                                          op0=ALU.max, op1=ALU.min))
            vm = V.tag(nc.vector.tensor_scalar(
                iq2_bf[:, ic * 512:(ic + 1) * 512], tscr[:, 512:1024], MAGIC,
                MAGIC, op0=ALU.add, op1=ALU.subtract))
            LS.wait(V, vm)
        iq2r = V.cnt
        ps_done(bank_r, V, iq2r)
        for hc in range(NCH):
            bank = hc % 2
            ps_wait(bank)
            for ic in range(NI1):
                rb = wstr_a if ic % 2 == 0 else wstr_b
                mw = load(rb[:, 0:128],
                          w2T_h[ic * 128:(ic + 1) * 128,
                                hc * 128:(hc + 1) * 128], q=LA)
                T_.wait(LA, mw)
                T_.wait(V, iq2r)
                T_.tag(MM(ps[bank][:, 0:512], rb[:, 0:128],
                          iq2_bf[:, ic * 512:(ic + 1) * 512],
                          start=(ic == 0), stop=(ic == NI1 - 1)))
                barrier_LA()
            V.wait(T_)
            V.tag(nc.vector.tensor_mul(tscr[:, 1024:1536], ps[bank][:, 0:512],
                                       s1b[:, 0:512]))
            cm = V.tag(nc.vector.scalar_tensor_tensor(
                x_sb[:, hc * T + t * 512: hc * T + (t + 1) * 512],
                tscr[:, 1024:1536], modv[(1, "gb")][:, hc:hc + 1],
                x_sb[:, hc * T + t * 512: hc * T + (t + 1) * 512],
                op0=ALU.mult, op1=ALU.add))
            ps_done(bank, V, cm)
    x_final = V.cnt
    dbg("xfin", x_sb[:, :])
    dbg("iq1", hb[:, :])

    T_.wait(V, x_final)
    smarks = []
    for tt in range(NTT):
        ob = outsb[:, 0:1024] if tt % 2 == 0 else outsb[:, 1024:2048]
        if tt >= 2:
            V.wait(LG, smarks[tt - 2])
        for dc in range(NCH):
            bank = 4 + (dc % 2)
            ps_wait(bank)
            T_.tag(nc.tensor.transpose(
                ps[bank][:, 0:128],
                x_sb[:, dc * T + tt * 128: dc * T + (tt + 1) * 128],
                ident[:, :]))
            V.wait(T_)
            cm = V.tag(nc.vector.tensor_copy(ob[:, dc * 128:(dc + 1) * 128],
                                             ps[bank][:, 0:128]))
            ps_done(bank, V, cm)
        G.wait(V)
        smarks.append(store(out_h[tt * 128:(tt + 1) * 128, :], ob))
    nc.gpsimd.wait_ge(LG.sem, LG.cnt)
    return nc


_NC_CACHE = None


def kernel(**ins):
    global _NC_CACHE
    if _NC_CACHE is None:
        _NC_CACHE = build()
    nc = _NC_CACHE

    f32 = np.float32
    x = np.ascontiguousarray(ins["L0_attn_x"], f32)
    half = 32
    freqs = 1.0 / 500.0 ** (np.arange(half, dtype=np.float64) / half)
    ang = np.arange(T, dtype=np.float64)[None, :] * freqs[:, None]
    cosv = np.cos(ang).astype(f32)
    sinv = np.sin(ang).astype(f32)
    cs64 = np.ascontiguousarray(np.concatenate([cosv, cosv], axis=0))
    sn64 = np.ascontiguousarray(np.concatenate([-sinv, sinv], axis=0))
    identb = np.eye(128, dtype=f32)
    constsb = np.zeros((128, NCONST), f32)
    constsb[:, C_LN127] = np.log(np.float32(127.0))
    constsb[:, C_SC0125] = 0.125
    constsb[:, C_NEG1] = -1.0
    constsb[:, C_NEGH] = -0.5
    constsb[:, C_LN2_26] = np.float32(26 * np.log(2.0))
    dmaskb = np.where(np.arange(128)[:, None] >= np.arange(128)[None, :],
                      np.float32(0.0), np.float32(-1e9)).astype(f32)

    wqkv = np.ascontiguousarray(ins["L1_attn_w_qkv"], f32)
    bqkv = np.ascontiguousarray(ins["L1_attn_b_qkv"], f32)
    swp = np.concatenate([np.arange(32, 64), np.arange(0, 32)])
    blocks = []
    bias_cols = []
    for h in range(H):
        qcols = wqkv[:, h * 64:(h + 1) * 64]
        kcols = wqkv[:, D + h * 64: D + (h + 1) * 64]
        blocks += [qcols, qcols[:, swp], kcols, kcols[:, swp]]
        bias_cols += [bqkv[h * 64:(h + 1) * 64],
                      bqkv[h * 64:(h + 1) * 64][swp],
                      bqkv[D + h * 64: D + (h + 1) * 64],
                      bqkv[D + h * 64: D + (h + 1) * 64][swp]]
    wqkvr = np.ascontiguousarray(np.concatenate(blocks, axis=1))
    biasr = np.ascontiguousarray(np.stack(bias_cols, axis=1))
    wv1 = np.ascontiguousarray(wqkv[:, 2 * D:3 * D])
    bvT = np.ascontiguousarray(bqkv[2 * D:3 * D].reshape(8, 128).T)

    common = {
        "wk": np.ascontiguousarray(ins["L0_attn_W_K"].reshape(D, D), f32),
        "wq": np.ascontiguousarray(ins["L0_attn_W_Q"].reshape(D, D), f32),
        "wv": np.ascontiguousarray(ins["L0_attn_W_V"].reshape(D, D), f32),
        "wo": np.ascontiguousarray(ins["L0_attn_W_O"], f32),
        "wg": np.ascontiguousarray(ins["L0_ffn_w_gate"], f32),
        "wu": np.ascontiguousarray(ins["L0_ffn_w_up"], f32),
        "wd": np.ascontiguousarray(ins["L0_ffn_w_down"], f32),
        "wmod0": np.ascontiguousarray(ins["L0_wmod"], f32),
        "wmod1": np.ascontiguousarray(ins["L1_wmod"], f32),
        "wqkvr": wqkvr,
        "wv1": wv1,
        "biasr_in": biasr,
        "bvT": bvT,
        "wout": np.ascontiguousarray(ins["L1_attn_w_out"], f32),
        "boutT": np.ascontiguousarray(
            ins["L1_attn_b_out"].reshape(8, 128).T, f32),
        "w1": np.ascontiguousarray(ins["L1_ffn_w1"], f32),
        "w2": np.ascontiguousarray(ins["L1_ffn_w2"], f32),
        "convwT": np.ascontiguousarray(
            ins["L1_ffn_conv_w"].reshape(NI1, 128, 3).transpose(1, 0, 2)
            .reshape(128, NI1 * 3), f32),
        "convbT": np.ascontiguousarray(
            ins["L1_ffn_conv_b"].reshape(NI1, 128).T, f32),
        "clipx": np.ascontiguousarray(ins["L0_ffn_clip_x"].reshape(1, 1), f32),
        "clipd": np.ascontiguousarray(ins["L0_ffn_clip_down"].reshape(1, 1),
                                      f32),
        "identh": identb,
        "constsh": constsb,
        "cs64h": cs64,
        "sn64h": sn64,
        "dmaskh": dmaskb,
    }
    in_maps = []
    for b in range(8):
        m = dict(common)
        m["x"] = np.ascontiguousarray(x[b])
        m["cb0"] = np.ascontiguousarray(ins["L0_c"][b:b + 1], f32)
        m["cb1"] = np.ascontiguousarray(ins["L1_c"][b:b + 1], f32)
        in_maps.append(m)

    res = run_bass_kernel_spmd(nc, in_maps, core_ids=list(range(8)))
    return np.stack([res.results[b]["out"] for b in range(8)], axis=0)


# version 06
# speedup vs baseline: 1.7929x; 1.7929x over previous
import numpy as np
import ml_dtypes
import concourse.bass as bass
import concourse.mybir as mybir
from concourse.bass_utils import run_bass_kernel_spmd

F32 = mybir.dt.float32
BF16 = mybir.dt.bfloat16
AF = mybir.ActivationFunctionType
ALU = mybir.AluOpType
AX = mybir.AxisListType

MAGIC = 12582912.0
T, D, H, DH = 2048, 1024, 16, 64
I0, I1 = 2816, 4096
NCH, NI0, NI1 = 8, 22, 32
NTC, NTT = 4, 16
BFnp = ml_dtypes.bfloat16
C_LN127, C_SC0125, C_NEG1, C_NEGH, C_LN2_26 = 0, 1, 2, 3, 4
NCONST = 5
KB = 1024
DEBUG = False

TRI = []
_o = 0
for _i in range(NTT):
    TRI.append(_o)
    _o += (_i // 4 + 1) * 512
NTRI = _o


class Eng:
    def __init__(self, k, name, eng, step=1):
        self.k, self.name, self.eng, self.step = k, name, eng, step
        self.sem = k.nc.semaphore("sem_" + name).__enter__()
        self.cnt = 0

    def tag(self, instr, n=1):
        self.cnt += self.step * n
        instr.then_inc(self.sem, self.step * n)
        return self.cnt

    def wait(self, prod, at=None):
        at = prod.cnt if at is None else at
        if at <= 0:
            return
        key = (self.name, prod.name)
        if self.k.waited.get(key, 0) >= at:
            return
        self.eng.wait_ge(prod.sem, at)
        self.k.waited[key] = at


def build():
    k = type("K", (), {})()
    nc = bass.Bass("TRN2", target_bir_lowering=False)
    k.nc = nc
    k.waited = {}
    T_ = Eng(k, "T", nc.tensor)
    V = Eng(k, "V", nc.vector)
    S = Eng(k, "S", nc.scalar)
    G = Eng(k, "G", nc.gpsimd)
    LS = Eng(k, "LS", nc.sync, step=16)
    LA = Eng(k, "LA", nc.scalar, step=16)
    LG = Eng(k, "LG", nc.gpsimd, step=16)
    CC = Eng(k, "CC", nc.gpsimd)

    def barrier_LS():
        LS.wait(T_); LS.wait(V); LS.wait(S); LS.wait(G); LS.wait(LA)

    def barrier_LA():
        LA.wait(T_); LA.wait(V)

    def barrier_CMP():
        for E in (T_, V, S):
            E.wait(LS); E.wait(LA); E.wait(LG)

    DT = nc.dram_tensor
    xin = DT("x", [T, D], F32, kind="ExternalInput")
    out_h = DT("out", [T, D], F32, kind="ExternalOutput")
    wk_h = DT("wk", [D, D], F32, kind="ExternalInput")
    wq_h = DT("wq", [D, D], F32, kind="ExternalInput")
    wv_h = DT("wv", [D, D], F32, kind="ExternalInput")
    wo_h = DT("wo", [D, D], F32, kind="ExternalInput")
    wg_h = DT("wg", [I0, D], F32, kind="ExternalInput")
    wu_h = DT("wu", [I0, D], F32, kind="ExternalInput")
    wd_h = DT("wd", [D, I0], F32, kind="ExternalInput")
    wmod_h = [DT("wmod0", [D, 6 * D], F32, kind="ExternalInput"),
              DT("wmod1", [D, 6 * D], F32, kind="ExternalInput")]
    cb_h = [DT("cb0", [1, D], F32, kind="ExternalInput"),
            DT("cb1", [1, D], F32, kind="ExternalInput")]
    wqkvr_h = DT("wqkvr", [D, H * 256], F32, kind="ExternalInput")
    wv1_h = DT("wv1", [D, D], F32, kind="ExternalInput")
    biasr_h = DT("biasr_in", [64, 64], F32, kind="ExternalInput")
    bv_h = DT("bvT", [128, 8], F32, kind="ExternalInput")
    wout_h = DT("wout", [D, D], F32, kind="ExternalInput")
    bout_h = DT("boutT", [128, 8], F32, kind="ExternalInput")
    w1_h = DT("w1", [I1, D], F32, kind="ExternalInput")
    w2_h = DT("w2", [D, I1], F32, kind="ExternalInput")
    convw_h = DT("convwT", [128, NI1 * 3], F32, kind="ExternalInput")
    convb_h = DT("convbT", [128, NI1], F32, kind="ExternalInput")
    clipx_h = DT("clipx", [1, 1], F32, kind="ExternalInput")
    clipd_h = DT("clipd", [1, 1], F32, kind="ExternalInput")
    ident_h = DT("identh", [128, 128], F32, kind="ExternalInput")
    consts_h = DT("constsh", [128, NCONST], F32, kind="ExternalInput")
    cs64_h = DT("cs64h", [64, T], F32, kind="ExternalInput")
    sn64_h = DT("sn64h", [64, T], F32, kind="ExternalInput")
    dmask_h = DT("dmaskh", [128, 128], F32, kind="ExternalInput")

    xspill_h = DT("xspill", [128, NCH * T], F32, kind="Internal")
    wrow_h = DT("wrow", [I1, D], BF16, kind="Internal")
    wkT_h = DT("wkT_s", [D, D], BF16, kind="Internal")
    wqT_h = DT("wqT_s", [D, D], BF16, kind="Internal")
    wvT_h = DT("wvT_s", [D, D], BF16, kind="Internal")
    wo_b = DT("wo_b", [D, D], BF16, kind="Internal")
    wg_b = DT("wg_b", [I0, D], BF16, kind="Internal")
    wu_b = DT("wu_b", [I0, D], BF16, kind="Internal")
    wd_b = DT("wd_b", [D, I0], BF16, kind="Internal")
    w1_b = DT("w1_b", [I1, D], BF16, kind="Internal")
    w2_b = DT("w2_b", [D, I1], BF16, kind="Internal")
    prod_h = DT("prod_s", [I0, T], F32, kind="Internal")
    y_h = DT("y_s", [I1, T], F32, kind="Internal")
    ccin_h = [DT(f"cc_in{i}", [1, 1], F32, kind="Internal") for i in range(2)]
    ccout_h = [DT(f"cc_out{i}", [1, 1], F32, kind="Internal") for i in range(2)]

    A = nc.alloc_sbuf_tensor
    ATo = nc.alloc_sbuf_tensor_at

    tscr = A("tscr", [128, 2048], F32)
    tscr2 = A("tscr2", [128, 2048], F32)
    ident = A("ident", [128, 128], F32)
    dmask = A("dmask", [128, 128], F32)
    consts = A("consts", [128, NCONST], F32)
    onesc = A("onesc", [128, 1], F32)
    onesc_b = A("onesc_b", [128, 1], BF16)
    ones1r = A("ones1r", [1, 128], F32)
    modv = {}
    for l in (0, 1):
        for nm in ("sa", "ca1", "ga", "sb", "cb1", "gb"):
            modv[(l, nm)] = A(f"mod_{l}_{nm}", [128, NCH], F32)
    swg = A("swg", [128, NI0], F32)
    swu = A("swu", [128, NI0], F32)
    swd = A("swd", [128, NCH], F32)
    swg_s = A("swg_s", [128, NI0], F32)
    swu_s = A("swu_s", [128, NI0], F32)
    swd_s = A("swd_s", [128, NCH], F32)
    swsc = A("swsc", [1, 4], F32)
    biasr = A("biasr", [64, 64], F32)
    bv = A("bv", [128, 8], F32)
    bout = A("bout", [128, 8], F32)
    convw = A("convw", [128, NI1 * 3], F32)
    convb = A("convb", [128, NI1], F32)
    clip_sb = A("clip_sb", [1, 2], F32)
    gsc = A("gsc", [1, 8], F32)
    c_row = A("c_row", [1, 2 * D], F32)
    scT_f = A("scT_f", [128, 2 * NCH], F32)
    scT_b = A("scT_b", [128, 2 * NCH], BF16)
    mod_raw = A("mod_raw", [128, 2 * 48], F32)
    modw_f = A("modw_f", [128, 2 * 1024], F32)
    modw_b = A("modw_b", [128, 2 * 1024], BF16)
    sc1 = A("sc1", [128, 1], F32)
    sc2 = A("sc2", [128, 1], F32)
    sc3 = A("sc3", [128, 1], F32)
    sc4 = A("sc4", [128, 1], F32)
    racc = A("racc", [128, 16], F32)
    rinv = A("rinv", [128, 16], F32)
    kn2c = A("kn2c", [128, 16], F32)
    mhc = A("mhc", [128, 16], F32)
    tmax = A("tmax", [128, 16], F32)
    gab = A("gab", [128, 8], F32)
    cqcol = A("cqcol", [128, NCH], F32)
    sqcol = A("sqcol", [128, NCH], F32)
    vt_bf = A("vt_bf", [128, 16 * 64], BF16)
    s1b = A("s1b", [128, 512], F32)

    x_sb = A("x_sb", [128, NCH * T], F32)
    hb = A("hb", [128, NCH * T], BF16)
    v_sb = A("v_sb", [128, 16 * 1040], BF16)
    z_sb = A("z_sb", [128, NCH * T], BF16)

    def addr(t):
        return nc.lookup_mloc(t).addr

    pa = addr(x_sb)
    at_bf = ATo("at_bf", [128, NTRI], BF16, offset=pa)
    nsq_a = ATo("nsq_a", [64, 2048], F32, offset=pa + 48 * KB)
    nsq_b = ATo("nsq_b", [64, 2048], F32, offset=pa + 56 * KB)
    sl1_bf = ATo("sl1_bf", [128, 16 * 512], BF16, offset=pa)
    cs64 = ATo("cs64", [64, 2048], F32, offset=pa + 16 * KB)
    sn64 = ATo("sn64", [64, 2048], F32, offset=pa + 24 * KB)
    nsq6 = ATo("nsq6", [64, 2048], F32, offset=pa + 32 * KB)
    rope_f = ATo("rope_f", [64, 2048], F32, offset=pa + 48 * KB)
    rope_g = ATo("rope_g", [64, 2048], F32, offset=pa + 56 * KB)
    kh_bf = ATo("kh_bf", [64, 2048], BF16, offset=pa + 40 * KB)
    qh_bf = ATo("qh_bf", [64, 2048], BF16, offset=pa + 44 * KB)

    pc = addr(v_sb)
    prodt_f = ATo("prodt_f", [128, 1024], F32, offset=pc + 22 * KB)

    pb = addr(hb)
    outsb = ATo("outsb", [128, 2048], F32, offset=pb)

    ps2a = addr(tscr2)
    wbuf = ATo("wbuf", [128, 2048], BF16, offset=ps2a)
    xbb = ATo("xbb", [128, 2048], BF16, offset=ps2a + 4 * KB)
    wfp = tscr

    pd = addr(z_sb)
    wstr_a = ATo("wstr_a", [128, 2048], BF16, offset=pd + 8 * KB)
    wstr_b = ATo("wstr_b", [128, 2048], BF16, offset=pd + 12 * KB)
    rowst = ATo("rowst", [1, 2048], F32, offset=pd + 16 * KB)
    r1row = ATo("r1row", [1, 2048], F32, offset=pd + 24 * KB)

    pf = [ATo("pf0", [128, 2048], F32, offset=pd),
          ATo("pf1", [128, 2048], F32, offset=pd + 8 * KB)]
    pbb = [ATo("pbb0", [128, 2048], BF16, offset=pd + 16 * KB),
           ATo("pbb1", [128, 2048], BF16, offset=pd + 20 * KB)]
    ttmp = ATo("ttmp", [128, 2048], F32, offset=pd + 24 * KB)
    spg = [ATo("spg0", [128, 4096], BF16, offset=pd),
           ATo("spg1", [128, 4096], BF16, offset=pd + 8 * KB)]
    spu = [ATo("spu0", [128, 4096], BF16, offset=pd + 16 * KB),
           ATo("spu1", [128, 4096], BF16, offset=pd + 24 * KB)]
    wdz = ATo("wdz", [128, 6 * 1024], BF16, offset=pd + 16 * KB)
    wdh = ATo("wdh", [128, 16 * 1024], BF16, offset=pb)
    spw1 = [ATo("spw10", [128, 4096], BF16, offset=pd),
            ATo("spw11", [128, 4096], BF16, offset=pd + 8 * KB)]
    w2z2 = ATo("w2z2", [128, 4 * 1024], BF16, offset=pd + 16 * KB)
    w2v = ATo("w2v", [128, 4 * 1024], BF16, offset=pc + 24 * KB)
    w2z = ATo("w2z", [128, 8 * 1024], BF16, offset=pd)
    w2h = ATo("w2h", [128, 16 * 1024], BF16, offset=pb)
    s1_all = ATo("s1_all", [128, 2048], F32, offset=pc)
    ymax_acc = ATo("ymax_acc", [128, 2048], F32, offset=pc + 8 * KB)
    rs_all = ATo("rs_all", [128, 2048], F32, offset=pc)
    ss_all = ATo("ss_all", [128, 2048], F32, offset=pc + 8 * KB)
    h1rB = ATo("h1rB", [128, 2048], F32, offset=pc + 16 * KB)
    yq = [ATo("yq0", [128, 512], F32, offset=pc + 16 * KB),
          ATo("yq1", [128, 512], F32, offset=pc + 18 * KB)]
    iqs = [ATo("iqs0", [128, 512], BF16, offset=pc + 20 * KB),
           ATo("iqs1", [128, 512], BF16, offset=pc + 21 * KB)]
    qmax = A("qmax", [128, 32], F32)
    qinv = A("qinv", [128, 32], F32)

    ps = [nc.alloc_psum_tensor(f"ps{i}", [128, 512], F32) for i in range(8)]
    ps_cons = [[] for _ in range(8)]

    def ps_wait(bank):
        for (q, c) in ps_cons[bank]:
            T_.wait(q, c)
        ps_cons[bank] = []

    def ps_done(bank, q, cnt=None):
        ps_cons[bank].append((q, cnt if cnt is not None else q.cnt))

    def load(dst, src, q=LS):
        return q.tag(q.eng.dma_start(dst, src))

    def loadT(dst, src):
        return LS.tag(nc.sync.dma_start(dst, src, transpose=True))

    def store(dst, src):
        return LG.tag(nc.gpsimd.dma_start(dst, src))

    MM = nc.tensor.matmul

    dbg_list = []

    def dbg(name, ap):
        if not DEBUG:
            return
        th = DT("d_" + name, list(ap.shape), ap.dtype, kind="ExternalOutput")
        G.wait(V)
        G.wait(S)
        G.wait(T_)
        store(th[tuple(slice(0, n) for n in ap.shape)], ap)
        V.wait(LG)
        S.wait(LG)
        T_.wait(LG)
        LS.wait(LG)
        LA.wait(LG)
        dbg_list.append(name)
    nc._dbg_list = dbg_list

    load(ident[:, :], ident_h[:, :])
    load(dmask[:, :], dmask_h[:, :])
    load(consts[:, :], consts_h[:, :])
    load(biasr[:, :], biasr_h[:, :])
    load(bv[:, :], bv_h[:, :])
    load(bout[:, :], bout_h[:, :])
    load(convw[:, :], convw_h[:, :])
    load(convb[:, :], convb_h[:, :])
    load(clip_sb[0:1, 0:1], clipx_h[:, :])
    load(clip_sb[0:1, 1:2], clipd_h[:, :])
    load(c_row[0:1, 0:D], cb_h[0][:, :])
    m_base = load(c_row[0:1, D:2 * D], cb_h[1][:, :])
    V.wait(LS, m_base)
    V.tag(nc.vector.memset(onesc[:, :], 1.0))
    V.tag(nc.vector.memset(onesc_b[:, :], 1.0))
    V.tag(nc.vector.memset(ones1r[:, :], 1.0))

    for tt in range(NTT):
        buf = tscr if tt % 2 == 0 else tscr2
        mk = load(buf[:, 0:1024], xin[tt * 128:(tt + 1) * 128, :], q=LA)
        T_.wait(LA, mk)
        T_.wait(LS, m_base)
        for dc in range(NCH):
            bank = 4 + (dc % 2)
            ps_wait(bank)
            T_.tag(nc.tensor.transpose(ps[bank][:, 0:128],
                                       buf[:, dc * 128:(dc + 1) * 128],
                                       ident[:, :]))
            V.wait(T_)
            cm = V.tag(nc.vector.tensor_copy(
                x_sb[:, dc * T + tt * 128: dc * T + (tt + 1) * 128],
                ps[bank][:, 0:128]))
            ps_done(bank, V, cm)
        barrier_LA()

    S.wait(LS, m_base)
    sm = S.tag(nc.scalar.activation(c_row[:, :], c_row[:, :], AF.Silu))
    T_.wait(S, sm)
    T_.wait(V)
    for l in (0, 1):
        for dc in range(NCH):
            bank = 6
            ps_wait(bank)
            T_.tag(MM(ps[bank][:, 0:1],
                      c_row[0:1, l * D + dc * 128: l * D + (dc + 1) * 128],
                      onesc[0:1, 0:1], start=True, stop=True))
            V.wait(T_)
            cm = V.tag(nc.vector.tensor_copy(
                scT_b[:, l * NCH + dc: l * NCH + dc + 1], ps[bank][:, 0:1]))
            ps_done(bank, V, cm)
    dbg("scTb_early", scT_b[:, :])

    for l in (0, 1):
        if l == 1:
            dbg("scTb_mid", scT_b[:, :])
        for j in range(48):
            half = (j % 2) * 1024
            buf = modw_f[:, half:half + 1024]
            bb = modw_b[:, half:half + 1024]
            barrier_LA()
            mk = 0
            for dc in range(NCH):
                mk = load(buf[:, dc * 128:(dc + 1) * 128],
                          wmod_h[l][dc * 128:(dc + 1) * 128,
                                    j * 128:(j + 1) * 128], q=LA)
            V.wait(LA, mk)
            V.wait(T_)
            vm = V.tag(nc.vector.tensor_copy(bb[:, 0:1024], buf[:, 0:1024]))
            bank = 6 + (j % 2)
            ps_wait(bank)
            T_.wait(V, vm)
            for dc in range(NCH):
                T_.tag(MM(ps[bank][:, 0:1], bb[:, dc * 128:(dc + 1) * 128],
                          scT_b[:, l * NCH + dc: l * NCH + dc + 1],
                          start=(dc == 0), stop=(dc == NCH - 1)))
            V.wait(T_)
            cm = V.tag(nc.vector.tensor_copy(
                mod_raw[:, l * 48 + j: l * 48 + j + 1], ps[bank][:, 0:1]))
            ps_done(bank, V, cm)
    for l in (0, 1):
        for vi, nm in enumerate(("sa", "ca1", "ga", "sb", "cb1", "gb")):
            src = mod_raw[:, l * 48 + vi * 8: l * 48 + (vi + 1) * 8]
            dst = modv[(l, nm)]
            if nm in ("ga", "gb"):
                V.tag(nc.vector.tensor_scalar(gab[:, 0:8], src, 0.0, 2.0,
                                              op0=ALU.is_ge, op1=ALU.mult))
                V.wait(V)
                V.tag(nc.vector.tensor_scalar_add(gab[:, 0:8], gab[:, 0:8],
                                                  -1.0))
                V.wait(V)
                vv = V.cnt
                S.wait(V, vv)
                sm = S.tag(nc.scalar.activation(scT_f[:, 0:8], src, AF.Abs))
                V.wait(S, sm)
                V.tag(nc.vector.scalar_tensor_tensor(
                    dst[:, :], scT_f[:, 0:8], 0.25, gab[:, 0:8],
                    op0=ALU.max, op1=ALU.mult))
            elif nm in ("ca1", "cb1"):
                V.tag(nc.vector.tensor_scalar_add(dst[:, :], src, 1.0))
            else:
                V.tag(nc.vector.tensor_copy(dst[:, :], src))
    mod_ready = V.cnt
    dbg("mod_raw", mod_raw[:, :])
    dbg("scTb", scT_b[:, :])
    dbg("x0", x_sb[:, :])

    def prep_cast(src_h, R, C, row_h, dstT_h):
        barrier_LS()
        for r in range(R // 128):
            mk = load(wfp[:, 0:C], src_h[r * 128:(r + 1) * 128, :])
            V.wait(LS, mk)
            V.wait(LG)
            vm = V.tag(nc.vector.tensor_copy(wbuf[:, 0:C], wfp[:, 0:C]))
            G.wait(V, vm)
            store(row_h[r * 128:(r + 1) * 128, 0:C], wbuf[:, 0:C])
            LS.wait(V, vm)
        st = LG.cnt
        LS.wait(LG, st)
        for e in range(C // 128):
            for r0 in range(0, R, 2048):
                rw = min(2048, R - r0)
                mk = loadT(xbb[:, 0:rw],
                           row_h[r0:r0 + rw, e * 128:(e + 1) * 128])
                G.wait(LS, mk)
                store(dstT_h[e * 128:(e + 1) * 128, r0:r0 + rw], xbb[:, 0:rw])
                LS.wait(LG)
        LS.wait(LG)

    def prep_cast2(src_h, R, C, dst_b):
        LA.wait(T_)
        LA.wait(V)
        LA.wait(S)
        pieces = [(c0, min(2048, C - c0)) for c0 in range(0, C, 2048)]
        it = 0
        vmh = {}
        sth = {}
        for r in range(R // 128):
            for (c0, w) in pieces:
                sl = it % 2
                if it >= 2:
                    LA.wait(V, vmh[it - 2])
                mk = load(pf[sl][:, 0:w],
                          src_h[r * 128:(r + 1) * 128, c0:c0 + w], q=LA)
                V.wait(LA, mk)
                if it >= 2:
                    V.wait(LG, sth[it - 2])
                vm = V.tag(nc.vector.tensor_copy(pbb[sl][:, 0:w],
                                                 pf[sl][:, 0:w]))
                vmh[it] = vm
                G.wait(V, vm)
                sth[it] = store(dst_b[r * 128:(r + 1) * 128, c0:c0 + w],
                                pbb[sl][:, 0:w])
                it += 1

    def prep_quant8_2(src_h, R, C, dst_b, scale_sb):
        ns = R // 128
        pieces = [(0, C)] if C <= 2048 else [(0, 2048), (2048, C - 2048)]
        LA.wait(T_)
        LA.wait(V)
        LA.wait(S)
        it = 0
        vmh = {}
        sth = {}
        for r in range(ns):
            for pi, (c0, w) in enumerate(pieces):
                sl = it % 2
                if it >= 2:
                    LA.wait(V, vmh[it - 2])
                mk = load(pf[sl][:, 0:w],
                          src_h[r * 128:(r + 1) * 128, c0:c0 + w], q=LA)
                V.wait(LA, mk)
                if pi == 0:
                    vm = V.tag(nc.vector.reduce_max(
                        qmax[:, r:r + 1], pf[sl][:, 0:w], axis=AX.X,
                        apply_absolute_value=True))
                else:
                    V.tag(nc.vector.reduce_max(sc2[:, :], pf[sl][:, 0:w],
                                               axis=AX.X,
                                               apply_absolute_value=True))
                    V.wait(V)
                    vm = V.tag(nc.vector.tensor_max(qmax[:, r:r + 1],
                                                    qmax[:, r:r + 1],
                                                    sc2[:, :]))
                vmh[it] = vm
                it += 1
        V.wait(V)
        S.wait(V)
        S.tag(nc.scalar.activation(qinv[:, 0:ns], qmax[:, 0:ns], AF.Ln))
        S.wait(S)
        sm = S.tag(nc.scalar.activation(qinv[:, 0:ns], qinv[:, 0:ns], AF.Exp,
                                        bias=consts[:, C_LN127:C_LN127 + 1],
                                        scale=consts[:, C_NEG1:C_NEG1 + 1]))
        V.wait(S, sm)
        V.tag(nc.vector.tensor_scalar_mul(scale_sb[:, 0:ns], qmax[:, 0:ns],
                                          1.0 / 127.0))
        LA.wait(V)
        it = 0
        for r in range(ns):
            for (c0, w) in pieces:
                sl = it % 2
                if it >= 2:
                    LA.wait(V, vmh[it - 2])
                mk = load(pf[sl][:, 0:w],
                          src_h[r * 128:(r + 1) * 128, c0:c0 + w], q=LA)
                S.wait(LA, mk)
                sm = S.tag(nc.scalar.activation(pf[sl][:, 0:w], pf[sl][:, 0:w],
                                                AF.Copy, bias=0.0,
                                                scale=qinv[:, r:r + 1]))
                V.wait(S, sm)
                V.tag(nc.vector.tensor_scalar(pf[sl][:, 0:w], pf[sl][:, 0:w],
                                              -128.0, 127.0, op0=ALU.max,
                                              op1=ALU.min))
                if it >= 2:
                    V.wait(LG, sth[it - 2])
                vm = V.tag(nc.vector.tensor_scalar(pbb[sl][:, 0:w],
                                                   pf[sl][:, 0:w], MAGIC,
                                                   MAGIC, op0=ALU.add,
                                                   op1=ALU.subtract))
                vmh[it] = vm
                G.wait(V, vm)
                sth[it] = store(dst_b[r * 128:(r + 1) * 128, c0:c0 + w],
                                pbb[sl][:, 0:w])
                it += 1

    def prep_ternary_2(src_h, R, C, dst_b, sw_slot):
        ns = R // 128
        pieces = [(0, C)] if C <= 2048 else [(0, 2048), (2048, C - 2048)]
        LA.wait(T_)
        LA.wait(V)
        LA.wait(S)
        it = 0
        vmh = {}
        sth = {}
        for r in range(ns):
            for pi, (c0, w) in enumerate(pieces):
                sl = it % 2
                if it >= 2:
                    LA.wait(V, vmh[it - 2])
                mk = load(pf[sl][:, 0:w],
                          src_h[r * 128:(r + 1) * 128, c0:c0 + w], q=LA)
                V.wait(LA, mk)
                if pi == 0:
                    vm = V.tag(nc.vector.reduce_sum(
                        qmax[:, r:r + 1], pf[sl][:, 0:w], axis=AX.X,
                        apply_absolute_value=True))
                else:
                    V.tag(nc.vector.reduce_sum(sc2[:, :], pf[sl][:, 0:w],
                                               axis=AX.X,
                                               apply_absolute_value=True))
                    V.wait(V)
                    vm = V.tag(nc.vector.tensor_add(qmax[:, r:r + 1],
                                                    qmax[:, r:r + 1],
                                                    sc2[:, :]))
                vmh[it] = vm
                it += 1
        V.wait(V)
        V.tag(nc.vector.reduce_sum(sc1[:, 0:1], qmax[:, 0:ns], axis=AX.X))
        bank = 6
        ps_wait(bank)
        T_.wait(V)
        T_.tag(MM(ps[bank][0:1, 0:1], sc1[:, :], onesc[:, :], start=True,
                  stop=True))
        V.wait(T_)
        V.tag(nc.vector.tensor_scalar_mul(gsc[0:1, 0:1], ps[bank][0:1, 0:1],
                                          1.0 / (R * C)))
        V.wait(V)
        V.tag(nc.vector.tensor_scalar_max(gsc[0:1, 0:1], gsc[0:1, 0:1], 1e-5))
        V.wait(V)
        vm = V.tag(nc.vector.tensor_scalar_mul(swsc[0:1, sw_slot:sw_slot + 1],
                                               gsc[0:1, 0:1], -1.0))
        ps_done(bank, V, vm)
        S.wait(V, vm)
        S.tag(nc.scalar.activation(gsc[0:1, 1:2], gsc[0:1, 0:1], AF.Ln))
        S.wait(S)
        sm = S.tag(nc.scalar.activation(gsc[0:1, 2:3], gsc[0:1, 1:2], AF.Exp,
                                        bias=0.0,
                                        scale=consts[0:1, C_NEG1:C_NEG1 + 1]))
        bank = 7
        ps_wait(bank)
        T_.wait(S, sm)
        T_.tag(MM(ps[bank][:, 0:1], ones1r[:, :], gsc[0:1, 2:3], start=True,
                  stop=True))
        V.wait(T_)
        vm = V.tag(nc.vector.tensor_copy(sc3[:, :], ps[bank][:, 0:1]))
        ps_done(bank, V, vm)
        S.wait(V, vm)
        LA.wait(V)
        it = 0
        for r in range(ns):
            for (c0, w) in pieces:
                sl = it % 2
                if it >= 2:
                    LA.wait(V, vmh[it - 2])
                mk = load(pf[sl][:, 0:w],
                          src_h[r * 128:(r + 1) * 128, c0:c0 + w], q=LA)
                S.wait(LA, mk)
                sm = S.tag(nc.scalar.activation(pf[sl][:, 0:w], pf[sl][:, 0:w],
                                                AF.Copy, bias=0.0,
                                                scale=sc3[:, 0:1]))
                V.wait(S, sm)
                V.tag(nc.vector.tensor_single_scalar(ttmp[:, 0:w],
                                                     pf[sl][:, 0:w], 0.5,
                                                     op=ALU.is_ge))
                if it >= 2:
                    V.wait(LG, sth[it - 2])
                vm = V.tag(nc.vector.scalar_tensor_tensor(
                    pbb[sl][:, 0:w], pf[sl][:, 0:w], -0.5, ttmp[:, 0:w],
                    op0=ALU.is_le, op1=ALU.subtract))
                vmh[it] = vm
                G.wait(V, vm)
                sth[it] = store(dst_b[r * 128:(r + 1) * 128, c0:c0 + w],
                                pbb[sl][:, 0:w])
                it += 1

    prep_cast(wk_h, D, D, wrow_h, wkT_h)
    prep_cast(wq_h, D, D, wrow_h, wqT_h)
    prep_cast(wv_h, D, D, wrow_h, wvT_h)
    prep_cast2(wo_h, D, D, wo_b)

    def ln_prep_t(src_sb, t):
        ps_wait(6)
        ps_wait(7)
        S.wait(T_)
        S.wait(V)
        for dc in range(NCH):
            sm = S.tag(nc.scalar.activation(
                tscr[:, 512:1024],
                src_sb[:, dc * T + t * 512: dc * T + (t + 1) * 512], AF.Square))
            T_.wait(S, sm)
            T_.tag(MM(ps[7][0:1, 0:512], onesc[:, :], tscr[:, 512:1024],
                      start=(dc == 0), stop=(dc == NCH - 1)))
            T_.tag(MM(ps[6][0:1, 0:512], onesc[:, :],
                      src_sb[:, dc * T + t * 512: dc * T + (t + 1) * 512],
                      start=(dc == 0), stop=(dc == NCH - 1)))
            if dc < NCH - 1:
                S.wait(T_)
        V.wait(T_)
        V.tag(nc.vector.tensor_scalar_mul(rowst[0:1, 0:512], ps[6][0:1, 0:512],
                                          1.0 / D))
        vm = V.tag(nc.vector.tensor_scalar_mul(rowst[0:1, 1536:2048],
                                               ps[7][0:1, 0:512], 1.0 / D))
        ps_done(6, V, vm)
        ps_done(7, V, vm)
        V.tag(nc.vector.tensor_mul(rowst[0:1, 512:1024], rowst[0:1, 0:512],
                                   rowst[0:1, 0:512]))
        V.tag(nc.vector.tensor_sub(rowst[0:1, 1536:2048],
                                   rowst[0:1, 1536:2048],
                                   rowst[0:1, 512:1024]))
        vm = V.tag(nc.vector.tensor_scalar_add(rowst[0:1, 1536:2048],
                                               rowst[0:1, 1536:2048], 1e-5))
        S.wait(V, vm)
        S.tag(nc.scalar.activation(rowst[0:1, 512:1024], rowst[0:1, 1536:2048],
                                   AF.Ln))
        S.wait(S)
        sm = S.tag(nc.scalar.activation(rowst[0:1, 512:1024],
                                        rowst[0:1, 512:1024], AF.Exp, bias=0.0,
                                        scale=consts[0:1, C_NEGH:C_NEGH + 1]))
        V.wait(S, sm)
        vm = V.tag(nc.vector.tensor_mul(rowst[0:1, 1024:1536],
                                        rowst[0:1, 0:512],
                                        rowst[0:1, 512:1024]))
        ps_wait(4)
        ps_wait(5)
        T_.wait(V, vm)
        T_.tag(MM(ps[4][:, 0:512], ones1r[:, :], rowst[0:1, 512:1024],
                  start=True, stop=True))
        T_.tag(MM(ps[5][:, 0:512], ones1r[:, :], rowst[0:1, 1024:1536],
                  start=True, stop=True))

    def ln_u(dst, src_sb, dc, t):
        V.wait(T_)
        V.tag(nc.vector.tensor_mul(
            dst, src_sb[:, dc * T + t * 512: dc * T + (t + 1) * 512],
            ps[4][:, 0:512]))
        return V.tag(nc.vector.tensor_sub(dst, dst, ps[5][:, 0:512]))

    V.wait(V, mod_ready)
    barrier_CMP()
    for t in range(NTC):
        ln_prep_t(x_sb, t)
        vm = 0
        for dc in range(NCH):
            vm = ln_u(tscr2[:, 0:512], x_sb, dc, t)
            S.wait(V, vm)
            sm = S.tag(nc.scalar.activation(
                hb[:, dc * T + t * 512: dc * T + (t + 1) * 512],
                tscr2[:, 0:512], AF.Identity,
                bias=modv[(0, "sa")][:, dc:dc + 1],
                scale=modv[(0, "ca1")][:, dc:dc + 1]))
            V.wait(S, sm)
        ps_done(4, V, vm)
        ps_done(5, V, vm)
    h_ready = S.cnt
    dbg("h_attn0", hb[:, :])

    G.wait(S, h_ready)
    G.wait(V)
    for dc in range(NCH):
        store(xspill_h[:, dc * T:(dc + 1) * T], x_sb[:, dc * T:(dc + 1) * T])
    xsp_done = LG.cnt

    barrier_LA()
    for ct in range(NTT):
        for nf in range(2):
            bank = nf
            ps_wait(bank)
            for dc in range(NCH):
                rb = wbuf if (dc % 2 == 0) else xbb
                mk = load(rb[:, 0:512],
                          wvT_h[dc * 128:(dc + 1) * 128,
                                nf * 512:(nf + 1) * 512], q=LA)
                T_.wait(LA, mk)
                T_.wait(S, h_ready)
                T_.tag(MM(ps[bank][:, 0:512],
                          hb[:, dc * T + ct * 128: dc * T + (ct + 1) * 128],
                          rb[:, 0:512], start=(dc == 0), stop=(dc == NCH - 1)))
                barrier_LA()
            V.wait(T_)
            cm = V.tag(nc.vector.tensor_copy(
                v_sb[:, ct * 1040 + nf * 512: ct * 1040 + nf * 512 + 512],
                ps[bank][:, 0:512]))
            ps_done(bank, V, cm)
    v_ready = V.cnt
    dbg("v0", v_sb[:, :])

    V.wait(LG, xsp_done)
    S.wait(LG, xsp_done)

    for h in range(H):
        par = h % 2
        for (mat_h, dst) in ((wkT_h, kh_bf), (wqT_h, qh_bf)):
            for t in range(NTC):
                bank = 2 + (t % 2)
                ps_wait(bank)
                for dc in range(NCH):
                    rb = wbuf if dc % 2 == 0 else xbb
                    mk = load(rb[:, 0:64],
                              mat_h[dc * 128:(dc + 1) * 128,
                                    h * 64:(h + 1) * 64], q=LA)
                    T_.wait(LA, mk)
                    T_.tag(MM(ps[bank][0:64, 0:512], rb[:, 0:64],
                              hb[:, dc * T + t * 512: dc * T + (t + 1) * 512],
                              start=(dc == 0), stop=(dc == NCH - 1)))
                    barrier_LA()
                V.wait(T_)
                cm = V.tag(nc.vector.tensor_copy(dst[:, t * 512:(t + 1) * 512],
                                                 ps[bank][0:64, 0:512]))
                ps_done(bank, V, cm)
        kq_ready = V.cnt

        S.wait(V, kq_ready)
        S.tag(nc.scalar.activation(nsq_a[:, :], kh_bf[:, :], AF.Square))
        sm2 = S.tag(nc.scalar.activation(nsq_b[:, :], qh_bf[:, :], AF.Square))
        T_.wait(S, sm2)
        V.tag(nc.vector.memset(sc1[:, :], 0.0))
        for ct in range(NTT):
            bank = 6 + (ct % 2)
            ps_wait(bank)
            T_.tag(MM(ps[bank][:, 0:1], nsq_a[:, ct * 128:(ct + 1) * 128],
                      onesc[0:64, :], start=True, stop=True))
            V.wait(T_)
            cm = V.tag(nc.vector.tensor_copy(kn2c[:, ct:ct + 1],
                                             ps[bank][:, 0:1]))
            ps_done(bank, V, cm)
        for ct in range(NTT):
            bank = 6 + (ct % 2)
            ps_wait(bank)
            T_.tag(MM(ps[bank][:, 0:1], nsq_b[:, ct * 128:(ct + 1) * 128],
                      onesc[0:64, :], start=True, stop=True))
            V.wait(T_)
            V.wait(V)
            cm = V.tag(nc.vector.tensor_max(sc1[:, :], sc1[:, :],
                                            ps[bank][:, 0:1]))
            ps_done(bank, V, cm)
        bank = 6
        ps_wait(bank)
        T_.wait(V)
        T_.tag(nc.tensor.transpose(ps[bank][0:1, 0:128], sc1[:, 0:1],
                                   ident[:, :]))
        V.wait(T_)
        vm = V.tag(nc.vector.reduce_max(gsc[0:1, 0:1], ps[bank][0:1, 0:128],
                                        axis=AX.X))
        ps_done(bank, V, vm)
        S.wait(V, vm)
        sm = S.tag(nc.scalar.activation(gsc[0:1, 1:2], gsc[0:1, 0:1], AF.Sqrt))
        bank = 7
        ps_wait(bank)
        T_.wait(S, sm)
        T_.tag(MM(ps[bank][:, 0:1], ones1r[:, :], gsc[0:1, 1:2], start=True,
                  stop=True))
        V.wait(T_)
        vm = V.tag(nc.vector.tensor_scalar_mul(sc2[:, :], ps[bank][:, 0:1],
                                               -0.125 * 1.0001))
        ps_done(bank, V, vm)
        S.wait(V, vm)
        sm = S.tag(nc.scalar.activation(mhc[:, :], kn2c[:, :], AF.Sqrt))
        V.wait(S, sm)
        V.wait(V)
        mh_done = V.tag(nc.vector.tensor_scalar(mhc[:, :], mhc[:, :],
                                                sc2[:, 0:1], None,
                                                op0=ALU.mult))
        S.wait(V, mh_done)

        V.tag(nc.vector.memset(racc[:, :], 0.0))
        for i in range(NTT):
            nj = i // 4 + 1
            for j in range(nj):
                bank = (i + j) % 2
                ps_wait(bank)
                T_.wait(V, kq_ready)
                wj = 512 if j < nj - 1 else (i % 4 + 1) * 128
                T_.tag(MM(ps[bank][0:128, 0:wj],
                          kh_bf[:, i * 128:(i + 1) * 128],
                          qh_bf[:, j * 512: j * 512 + wj], start=True,
                          stop=True))
                diag = (j == nj - 1)
                if diag:
                    sub = i % 4
                    V.wait(T_)
                    V.tag(nc.vector.tensor_add(
                        ps[bank][:, sub * 128:(sub + 1) * 128],
                        ps[bank][:, sub * 128:(sub + 1) * 128], dmask[:, :]))
                S.wait(T_)
                S.wait(V)
                sm = S.tag(nc.scalar.activation(
                    at_bf[:, TRI[i] + j * 512: TRI[i] + j * 512 + wj],
                    ps[bank][:, 0:wj], AF.Exp, bias=mhc[:, i:i + 1],
                    scale=consts[:, C_SC0125:C_SC0125 + 1],
                    accum_out=sc1[:, 0:1]))
                ps_done(bank, S, sm)
                V.wait(S, sm)
                V.tag(nc.vector.tensor_add(racc[:, i:i + 1], racc[:, i:i + 1],
                                           sc1[:, 0:1]))
                if wj < 512:
                    V.tag(nc.vector.memset(
                        at_bf[:, TRI[i] + j * 512 + wj: TRI[i] + (j + 1) * 512],
                        0.0))
        V.wait(V)
        vm = V.tag(nc.vector.tensor_scalar(rinv[:, :], racc[:, :],
                                           2.0 ** -96, 2.0 ** 26,
                                           op0=ALU.max, op1=ALU.mult))
        S.wait(V, vm)
        S.tag(nc.scalar.activation(rinv[:, :], rinv[:, :], AF.Ln))
        S.wait(S)
        sm = S.tag(nc.scalar.activation(rinv[:, :], rinv[:, :], AF.Exp,
                                        bias=consts[:, C_LN2_26:C_LN2_26 + 1],
                                        scale=consts[:, C_NEG1:C_NEG1 + 1]))
        V.wait(S, sm)
        V.wait(V, v_ready)
        for ct in range(NTT):
            V.tag(nc.vector.tensor_scalar(
                vt_bf[:, ct * 64:(ct + 1) * 64],
                v_sb[:, ct * 1040 + h * 64: ct * 1040 + h * 64 + 64],
                rinv[:, ct:ct + 1], None, op0=ALU.mult))
        vt_done = V.cnt
        if h == 0:
            dbg("at_h0", at_bf[:, :])
            dbg("vt_h0", vt_bf[:, :])
            dbg("racc_h0", racc[:, :])
            dbg("rinv_h0", rinv[:, :])
            dbg("mhc_h0", mhc[:, :])

        for j in range(NTC):
            bank = 2 + (j % 2)
            ps_wait(bank)
            T_.wait(V, vt_done)
            T_.wait(G) if (h == 0 and DEBUG) else None
            tp = None if par == 0 else (0, 64)
            rows = slice(0, 64) if par == 0 else slice(64, 128)
            for i in range(4 * j, NTT):
                T_.tag(MM(ps[bank][rows, 0:512], vt_bf[:, i * 64:(i + 1) * 64],
                          at_bf[:, TRI[i] + j * 512: TRI[i] + (j + 1) * 512],
                          start=(i == 4 * j), stop=(i == NTT - 1),
                          tile_position=tp))
            V.wait(T_)
            cm = V.tag(nc.vector.tensor_copy(
                z_sb[rows, (h // 2) * T + j * 512: (h // 2) * T + (j + 1) * 512],
                ps[bank][rows, 0:512]))
            ps_done(bank, V, cm)
    z_ready = V.cnt
    dbg("z0", z_sb[:, :])
    dbg("kh15", kh_bf[:, :])
    dbg("qh15", qh_bf[:, :])
    dbg("racc15", racc[:, :])
    dbg("mhc15", mhc[:, :])

    LS.wait(T_)
    LS.wait(V, z_ready)
    LS.wait(S)
    LA.wait(T_)
    LA.wait(V, z_ready)
    LA.wait(S)
    wo_marks = [0] * NCH
    wo_cons = [0] * NCH
    xm0 = [0] * NCH

    def issue_wo(e):
        pan = wbuf[:, 0:1024] if e % 2 == 0 else wbuf[:, 1024:2048]
        if e >= 2:
            LS.wait(T_, wo_cons[e - 2])
        mk = 0
        for fc in range(NCH):
            mk = loadT(pan[:, fc * 128:(fc + 1) * 128],
                       wo_b[e * 128:(e + 1) * 128, fc * 128:(fc + 1) * 128])
        wo_marks[e] = mk
        xm0[e] = load(x_sb[:, e * T:(e + 1) * T],
                      xspill_h[:, e * T:(e + 1) * T], q=LA)

    issue_wo(0)
    issue_wo(1)
    for e in range(NCH):
        pan = wbuf[:, 0:1024] if e % 2 == 0 else wbuf[:, 1024:2048]
        T_.wait(LS, wo_marks[e])
        for t in range(NTC):
            bank = t % 2
            ps_wait(bank)
            for fc in range(NCH):
                T_.tag(MM(ps[bank][:, 0:512], pan[:, fc * 128:(fc + 1) * 128],
                          z_sb[:, fc * T + t * 512: fc * T + (t + 1) * 512],
                          start=(fc == 0), stop=(fc == NCH - 1)))
            V.wait(T_)
            V.wait(LA, xm0[e])
            cm = V.tag(nc.vector.scalar_tensor_tensor(
                x_sb[:, e * T + t * 512: e * T + (t + 1) * 512],
                ps[bank][:, 0:512], modv[(0, "ga")][:, e:e + 1],
                x_sb[:, e * T + t * 512: e * T + (t + 1) * 512],
                op0=ALU.mult, op1=ALU.add))
            ps_done(bank, V, cm)
        wo_cons[e] = T_.cnt
        if e + 2 < NCH:
            issue_wo(e + 2)
    x_attn0 = V.cnt
    dbg("xa0", x_sb[:, :])

    prep_quant8_2(wg_h, I0, D, wg_b, swg)
    prep_quant8_2(wu_h, I0, D, wu_b, swu)
    prep_quant8_2(wd_h, D, I0, wd_b, swd)

    V.wait(V, x_attn0)
    barrier_CMP()
    V.tag(nc.vector.memset(sc1[:, :], 0.0))
    for t in range(NTC):
        ln_prep_t(x_sb, t)
        vm = 0
        for dc in range(NCH):
            vm = ln_u(tscr2[:, 0:512], x_sb, dc, t)
            S.wait(V, vm)
            sm = S.tag(nc.scalar.activation(tscr[:, 0:512], tscr2[:, 0:512],
                                            AF.Identity,
                                            bias=modv[(0, "sb")][:, dc:dc + 1],
                                            scale=modv[(0, "cb1")][:, dc:dc + 1]))
            V.wait(S, sm)
            V.tag(nc.vector.reduce_max(sc2[:, :], tscr[:, 0:512], axis=AX.X,
                                       apply_absolute_value=True))
            V.wait(V)
            vm = V.tag(nc.vector.tensor_max(sc1[:, :], sc1[:, :], sc2[:, :]))
        ps_done(4, V, vm)
        ps_done(5, V, vm)
    bank = 6
    ps_wait(bank)
    T_.wait(V)
    T_.tag(nc.tensor.transpose(ps[bank][0:1, 0:128], sc1[:, 0:1], ident[:, :]))
    V.wait(T_)
    vm = V.tag(nc.vector.reduce_max(gsc[0:1, 0:1], ps[bank][0:1, 0:128],
                                    axis=AX.X))
    ps_done(bank, V, vm)
    LS.wait(V, vm)
    ccm = load(ccin_h[0][:, :], gsc[0:1, 0:1])
    G.wait(LS, ccm)
    cc1 = CC.tag(nc.gpsimd.collective_compute(
        "AllReduce", ALU.max, ins=[ccin_h[0][:, :]], outs=[ccout_h[0][:, :]],
        replica_groups=[list(range(8))]))
    LS.wait(CC, cc1)
    gmk = load(gsc[0:1, 2:3], ccout_h[0][:, :])
    S.wait(LS, gmk)
    sm = S.tag(nc.scalar.activation(gsc[0:1, 3:4], clip_sb[0:1, 0:1],
                                    AF.Sigmoid))
    V.wait(S, sm)
    V.tag(nc.vector.tensor_mul(gsc[0:1, 3:4], gsc[0:1, 3:4], gsc[0:1, 2:3]))
    V.wait(V)
    vm = V.tag(nc.vector.tensor_scalar_mul(gsc[0:1, 6:7], gsc[0:1, 3:4],
                                           1.0 / 127.0))
    S.wait(V, vm)
    S.tag(nc.scalar.activation(gsc[0:1, 4:5], gsc[0:1, 3:4], AF.Ln))
    S.wait(S)
    sm = S.tag(nc.scalar.activation(gsc[0:1, 5:6], gsc[0:1, 4:5], AF.Exp,
                                    bias=consts[0:1, C_LN127:C_LN127 + 1],
                                    scale=consts[0:1, C_NEG1:C_NEG1 + 1]))
    bank = 7
    ps_wait(bank)
    T_.wait(S, sm)
    T_.wait(V, vm)
    T_.tag(MM(ps[bank][:, 0:1], ones1r[:, :], gsc[0:1, 5:6], start=True,
              stop=True))
    V.wait(T_)
    cm = V.tag(nc.vector.tensor_copy(sc3[:, :], ps[bank][:, 0:1]))
    ps_done(bank, V, cm)
    bank = 6
    ps_wait(bank)
    T_.tag(MM(ps[bank][:, 0:1], ones1r[:, :], gsc[0:1, 6:7], start=True,
              stop=True))
    V.wait(T_)
    cm = V.tag(nc.vector.tensor_copy(sc4[:, :], ps[bank][:, 0:1]))
    ps_done(bank, V, cm)
    V.wait(V)
    for dc in range(NCH):
        V.tag(nc.vector.tensor_scalar(cqcol[:, dc:dc + 1],
                                      modv[(0, "cb1")][:, dc:dc + 1],
                                      sc3[:, 0:1], None, op0=ALU.mult))
        V.tag(nc.vector.tensor_scalar(sqcol[:, dc:dc + 1],
                                      modv[(0, "sb")][:, dc:dc + 1],
                                      sc3[:, 0:1], None, op0=ALU.mult))
    V.tag(nc.vector.tensor_scalar(swg_s[:, :], swg[:, :], sc4[:, 0:1], None,
                                  op0=ALU.mult))
    vm = V.tag(nc.vector.tensor_scalar(swu_s[:, :], swu[:, :], sc4[:, 0:1],
                                       None, op0=ALU.mult))
    for t in range(NTC):
        ln_prep_t(x_sb, t)
        vm = 0
        for dc in range(NCH):
            vm = ln_u(tscr2[:, 0:512], x_sb, dc, t)
            S.wait(V, vm)
            sm = S.tag(nc.scalar.activation(tscr[:, 0:512], tscr2[:, 0:512],
                                            AF.Identity,
                                            bias=sqcol[:, dc:dc + 1],
                                            scale=cqcol[:, dc:dc + 1]))
            V.wait(S, sm)
            V.tag(nc.vector.tensor_scalar(tscr[:, 0:512], tscr[:, 0:512],
                                          -128.0, 127.0, op0=ALU.max,
                                          op1=ALU.min))
            vm = V.tag(nc.vector.tensor_scalar(
                hb[:, dc * T + t * 512: dc * T + (t + 1) * 512],
                tscr[:, 0:512], MAGIC, MAGIC, op0=ALU.add, op1=ALU.subtract))
        ps_done(4, V, vm)
        ps_done(5, V, vm)
    iq_ready = V.cnt
    dbg("iq0", hb[:, :])
    dbg("gsc4", gsc[:, :])
    dbg("swg", swg[:, :])
    dbg("swgs", swg_s[:, :])

    V.tag(nc.vector.memset(sc1[:, :], 0.0))
    NSP0 = (NI0 + 3) // 4
    sp0_marks = [0] * NSP0
    sp0_cons = [0] * NSP0

    def issue_sp0(sp):
        if sp >= 2:
            LS.wait(T_, sp0_cons[sp - 2])
        n = min(4, NI0 - sp * 4)
        mk = 0
        for dc in range(NCH):
            mk = loadT(spg[sp % 2][:, dc * 512: dc * 512 + n * 128],
                       wg_b[sp * 512: sp * 512 + n * 128,
                            dc * 128:(dc + 1) * 128])
            mk = loadT(spu[sp % 2][:, dc * 512: dc * 512 + n * 128],
                       wu_b[sp * 512: sp * 512 + n * 128,
                            dc * 128:(dc + 1) * 128])
        sp0_marks[sp] = mk

    LS.wait(T_)
    LS.wait(V)
    LS.wait(S)
    issue_sp0(0)
    issue_sp0(1)
    T_.wait(V, iq_ready)
    it5 = 0
    vm5 = {}
    pst5 = {}
    for sp in range(NSP0):
        T_.wait(LS, sp0_marks[sp])
        for ic in range(sp * 4, min((sp + 1) * 4, NI0)):
            ii = ic - sp * 4
            for t in range(NTC):
                bg, bu = t, 4 + t
                ps_wait(bg)
                ps_wait(bu)
                for dc in range(NCH):
                    T_.tag(MM(ps[bg][:, 0:512],
                              spg[sp % 2][:, dc * 512 + ii * 128:
                                          dc * 512 + ii * 128 + 128],
                              hb[:, dc * T + t * 512: dc * T + (t + 1) * 512],
                              start=(dc == 0), stop=(dc == NCH - 1)))
                    T_.tag(MM(ps[bu][:, 0:512],
                              spu[sp % 2][:, dc * 512 + ii * 128:
                                          dc * 512 + ii * 128 + 128],
                              hb[:, dc * T + t * 512: dc * T + (t + 1) * 512],
                              start=(dc == 0), stop=(dc == NCH - 1)))
                off = (it5 % 2) * 1024
                S.wait(T_)
                if it5 >= 2:
                    S.wait(V, vm5[it5 - 2])
                sm = S.tag(nc.scalar.activation(tscr[:, off:off + 512],
                                                ps[bg][:, 0:512], AF.Silu,
                                                bias=0.0,
                                                scale=swg_s[:, ic:ic + 1]))
                ps_done(bg, S, sm)
                V.wait(S, sm)
                V.tag(nc.vector.tensor_scalar(tscr[:, off + 512:off + 1024],
                                              ps[bu][:, 0:512],
                                              swu_s[:, ic:ic + 1], None,
                                              op0=ALU.mult))
                pb_ = (it5 % 2) * 512
                if it5 >= 2:
                    V.wait(LG, pst5[it5 - 2])
                vm = V.tag(nc.vector.tensor_mul(prodt_f[:, pb_:pb_ + 512],
                                                tscr[:, off:off + 512],
                                                tscr[:, off + 512:off + 1024]))
                ps_done(bu, V, vm)
                vm5[it5] = vm
                V.tag(nc.vector.reduce_max(sc2[:, :], prodt_f[:, pb_:pb_ + 512],
                                           axis=AX.X,
                                           apply_absolute_value=True))
                V.wait(V)
                V.tag(nc.vector.tensor_max(sc1[:, :], sc1[:, :], sc2[:, :]))
                G.wait(V)
                pst5[it5] = store(prod_h[ic * 128:(ic + 1) * 128,
                                         t * 512:(t + 1) * 512],
                                  prodt_f[:, pb_:pb_ + 512])
                it5 += 1
        sp0_cons[sp] = T_.cnt
        if sp + 2 < NSP0:
            issue_sp0(sp + 2)

    gu_T_done = T_.cnt
    LS.wait(T_, gu_T_done)
    for ic in range(NI0):
        dstb = (wdh[:, ic * 1024:(ic + 1) * 1024] if ic < 16
                else wdz[:, (ic - 16) * 1024:(ic - 15) * 1024])
        loadT(dstb, wd_b[0:D, ic * 128:(ic + 1) * 128])
    wd_loaded = LS.cnt
    bank = 6
    ps_wait(bank)
    T_.wait(V)
    T_.tag(nc.tensor.transpose(ps[bank][0:1, 0:128], sc1[:, 0:1], ident[:, :]))
    V.wait(T_)
    vm = V.tag(nc.vector.reduce_max(gsc[0:1, 0:1], ps[bank][0:1, 0:128],
                                    axis=AX.X))
    ps_done(bank, V, vm)
    LA.wait(V, vm)
    ccm = load(ccin_h[1][:, :], gsc[0:1, 0:1], q=LA)
    G.wait(LA, ccm)
    G.wait(LG)
    cc2 = CC.tag(nc.gpsimd.collective_compute(
        "AllReduce", ALU.max, ins=[ccin_h[1][:, :]], outs=[ccout_h[1][:, :]],
        replica_groups=[list(range(8))]))
    LA.wait(CC, cc2)
    gmk = load(gsc[0:1, 2:3], ccout_h[1][:, :], q=LA)
    S.wait(LA, gmk)
    sm = S.tag(nc.scalar.activation(gsc[0:1, 3:4], clip_sb[0:1, 1:2],
                                    AF.Sigmoid))
    V.wait(S, sm)
    V.tag(nc.vector.tensor_mul(gsc[0:1, 3:4], gsc[0:1, 3:4], gsc[0:1, 2:3]))
    V.wait(V)
    vm = V.tag(nc.vector.tensor_scalar_mul(gsc[0:1, 6:7], gsc[0:1, 3:4],
                                           1.0 / 127.0))
    S.wait(V, vm)
    S.tag(nc.scalar.activation(gsc[0:1, 4:5], gsc[0:1, 3:4], AF.Ln))
    S.wait(S)
    sm = S.tag(nc.scalar.activation(gsc[0:1, 5:6], gsc[0:1, 4:5], AF.Exp,
                                    bias=consts[0:1, C_LN127:C_LN127 + 1],
                                    scale=consts[0:1, C_NEG1:C_NEG1 + 1]))
    bank = 7
    ps_wait(bank)
    T_.wait(S, sm)
    T_.wait(V, vm)
    T_.tag(MM(ps[bank][:, 0:1], ones1r[:, :], gsc[0:1, 5:6], start=True,
              stop=True))
    V.wait(T_)
    cm = V.tag(nc.vector.tensor_copy(sc3[:, :], ps[bank][:, 0:1]))
    ps_done(bank, V, cm)
    bank = 6
    ps_wait(bank)
    T_.tag(MM(ps[bank][:, 0:1], ones1r[:, :], gsc[0:1, 6:7], start=True,
              stop=True))
    V.wait(T_)
    cm = V.tag(nc.vector.tensor_copy(sc4[:, :], ps[bank][:, 0:1]))
    ps_done(bank, V, cm)
    V.wait(V)
    for dc in range(NCH):
        V.tag(nc.vector.tensor_scalar(swd_s[:, dc:dc + 1], swd[:, dc:dc + 1],
                                      sc4[:, 0:1], None, op0=ALU.mult))
        V.wait(V)
        V.tag(nc.vector.tensor_mul(swd_s[:, dc:dc + 1], swd_s[:, dc:dc + 1],
                                   modv[(0, "gb")][:, dc:dc + 1]))
        V.wait(V)
    rq2_ready = V.cnt

    def wd_stripe(ic):
        return (wdh[:, ic * 1024:(ic + 1) * 1024] if ic < 16
                else wdz[:, (ic - 16) * 1024:(ic - 15) * 1024])

    itd = 0
    qm_d = {}
    cons_d = {}
    for t in range(NTC):
        for hc in range(NCH):
            ps_wait(hc)
        for ic in range(NI0):
            sl = itd % 2
            if itd >= 2:
                LA.wait(V, qm_d[itd - 2])
            mk = load(yq[sl][:, 0:512],
                      prod_h[ic * 128:(ic + 1) * 128, t * 512:(t + 1) * 512],
                      q=LA)
            V.wait(LA, mk)
            if itd >= 2:
                V.wait(T_, cons_d[itd - 2])
            V.tag(nc.vector.tensor_scalar(tscr[:, sl * 512:sl * 512 + 512],
                                          yq[sl][:, 0:512], sc3[:, 0:1], None,
                                          op0=ALU.mult))
            V.tag(nc.vector.tensor_scalar(tscr[:, sl * 512:sl * 512 + 512],
                                          tscr[:, sl * 512:sl * 512 + 512],
                                          -128.0, 127.0, op0=ALU.max,
                                          op1=ALU.min))
            qm = V.tag(nc.vector.tensor_scalar(
                iqs[sl][:, 0:512], tscr[:, sl * 512:sl * 512 + 512], MAGIC,
                MAGIC, op0=ALU.add, op1=ALU.subtract))
            qm_d[itd] = qm
            T_.wait(LS, wd_loaded)
            T_.wait(V, qm)
            for hc in range(NCH):
                T_.tag(MM(ps[hc][:, 0:512],
                          wd_stripe(ic)[:, hc * 128:(hc + 1) * 128],
                          iqs[sl][:, 0:512], start=(ic == 0),
                          stop=(ic == NI0 - 1)))
            cons_d[itd] = T_.cnt
            itd += 1
        V.wait(T_)
        for hc in range(NCH):
            cm = V.tag(nc.vector.scalar_tensor_tensor(
                x_sb[:, hc * T + t * 512: hc * T + (t + 1) * 512],
                ps[hc][:, 0:512], swd_s[:, hc:hc + 1],
                x_sb[:, hc * T + t * 512: hc * T + (t + 1) * 512],
                op0=ALU.mult, op1=ALU.add))
            ps_done(hc, V, cm)
    x_ffn0 = V.cnt
    dbg("xf0", x_sb[:, :])

    prep_ternary_2(w1_h, I1, D, w1_b, 0)
    prep_ternary_2(w2_h, D, I1, w2_b, 1)

    V.wait(V, x_ffn0)
    barrier_CMP()
    for t in range(NTC):
        ln_prep_t(x_sb, t)
        vm = 0
        for dc in range(NCH):
            vm = ln_u(tscr2[:, 0:512], x_sb, dc, t)
            S.wait(V, vm)
            sm = S.tag(nc.scalar.activation(
                hb[:, dc * T + t * 512: dc * T + (t + 1) * 512],
                tscr2[:, 0:512], AF.Identity,
                bias=modv[(1, "sa")][:, dc:dc + 1],
                scale=modv[(1, "ca1")][:, dc:dc + 1]))
            V.wait(S, sm)
        ps_done(4, V, vm)
        ps_done(5, V, vm)
    h_ready = S.cnt
    G.wait(S, h_ready)
    G.wait(V)
    for dc in range(NCH):
        store(xspill_h[:, dc * T:(dc + 1) * T], x_sb[:, dc * T:(dc + 1) * T])
    xsp_done = LG.cnt
    LS.wait(LG, xsp_done)
    load(cs64[:, :], cs64_h[:, :])
    rope_ready = load(sn64[:, :], sn64_h[:, :])

    barrier_LA()
    for ct in range(NTT):
        for nf in range(2):
            bank = nf
            ps_wait(bank)
            for dc in range(NCH):
                mk = load(tscr2[:, 1024:1536],
                          wv1_h[dc * 128:(dc + 1) * 128,
                                nf * 512:(nf + 1) * 512], q=LA)
                V.wait(LA, mk)
                V.wait(T_)
                vm = V.tag(nc.vector.tensor_copy(wstr_a[:, 0:512],
                                                 tscr2[:, 1024:1536]))
                T_.wait(V, vm)
                T_.wait(S, h_ready)
                T_.tag(MM(ps[bank][:, 0:512],
                          hb[:, dc * T + ct * 128: dc * T + (ct + 1) * 128],
                          wstr_a[:, 0:512], start=(dc == 0),
                          stop=(dc == NCH - 1)))
                barrier_LA()
                LA.wait(V, vm)
            V.wait(T_)
            vm = V.tag(nc.vector.tensor_copy(
                v_sb[:, ct * 1040 + nf * 512: ct * 1040 + nf * 512 + 512],
                ps[bank][:, 0:512]))
            ps_done(bank, V, vm)
    for c in range(8):
        bank = 6
        ps_wait(bank)
        T_.wait(LS, m_base)
        T_.wait(V)
        T_.tag(nc.tensor.transpose(ps[bank][0:1, 0:128], bv[:, c:c + 1],
                                   ident[:, :]))
        V.wait(T_)
        cm = V.tag(nc.vector.tensor_copy(
            tscr[0:1, 1024 + c * 128: 1024 + (c + 1) * 128],
            ps[bank][0:1, 0:128]))
        ps_done(bank, V, cm)
    for nf in range(2):
        bank = 6 + nf
        ps_wait(bank)
        T_.wait(V)
        T_.tag(MM(ps[bank][:, 0:512], ones1r[:, :],
                  tscr[0:1, 1024 + nf * 512: 1024 + (nf + 1) * 512],
                  start=True, stop=True))
    V.wait(T_)
    vm = 0
    for ct in range(NTT):
        for nf in range(2):
            vm = V.tag(nc.vector.tensor_add(
                v_sb[:, ct * 1040 + nf * 512: ct * 1040 + nf * 512 + 512],
                v_sb[:, ct * 1040 + nf * 512: ct * 1040 + nf * 512 + 512],
                ps[6 + nf][:, 0:512]))
    ps_done(6, V, vm)
    ps_done(7, V, vm)
    v_ready = V.cnt

    for h in range(H):
        pc_ = h // 2
        par = h % 2
        for (jq, dst) in ((0, qh_bf), (2, kh_bf)):
            for t in range(NTC):
                bA = 2 + (t % 2)
                bB = 0 + (t % 2)
                ps_wait(bA)
                ps_wait(bB)
                for dc in range(NCH):
                    mk = load(tscr2[:, 1024:1152],
                              wqkvr_h[dc * 128:(dc + 1) * 128,
                                      h * 256 + jq * 64:
                                      h * 256 + jq * 64 + 128], q=LA)
                    V.wait(LA, mk)
                    V.wait(T_)
                    vm = V.tag(nc.vector.tensor_copy(wstr_a[:, 512:640],
                                                     tscr2[:, 1024:1152]))
                    T_.wait(V, vm)
                    T_.tag(MM(ps[bA][0:64, 0:512], wstr_a[:, 512:576],
                              hb[:, dc * T + t * 512: dc * T + (t + 1) * 512],
                              start=(dc == 0), stop=(dc == NCH - 1)))
                    T_.tag(MM(ps[bB][0:64, 0:512], wstr_a[:, 576:640],
                              hb[:, dc * T + t * 512: dc * T + (t + 1) * 512],
                              start=(dc == 0), stop=(dc == NCH - 1)))
                    barrier_LA()
                    LA.wait(V, vm)
                S.wait(T_)
                S.wait(V)
                sm = S.tag(nc.scalar.activation(
                    rope_f[:, t * 512:(t + 1) * 512], ps[bA][0:64, 0:512],
                    AF.Identity, bias=biasr[:, h * 4 + jq: h * 4 + jq + 1],
                    scale=1.0))
                ps_done(bA, S, sm)
                sm = S.tag(nc.scalar.activation(
                    rope_g[:, t * 512:(t + 1) * 512], ps[bB][0:64, 0:512],
                    AF.Identity, bias=biasr[:, h * 4 + jq + 1: h * 4 + jq + 2],
                    scale=1.0))
                ps_done(bB, S, sm)
                sm = S.tag(nc.scalar.activation(
                    nsq6[:, t * 512:(t + 1) * 512],
                    rope_f[:, t * 512:(t + 1) * 512], AF.Square))
            V.wait(S)
            V.wait(LS, rope_ready)
            for t in range(NTC):
                sl_ = slice(t * 512, (t + 1) * 512)
                V.tag(nc.vector.tensor_mul(tscr[0:64, 0:512], rope_f[:, sl_],
                                           cs64[:, sl_]))
                V.tag(nc.vector.tensor_mul(tscr[0:64, 512:1024],
                                           rope_g[:, sl_], sn64[:, sl_]))
                V.tag(nc.vector.tensor_add(dst[:, sl_], tscr[0:64, 0:512],
                                           tscr[0:64, 512:1024]))
            T_.wait(S)
            V.tag(nc.vector.memset(sc1[:, :], 0.0))
            for ct in range(NTT):
                bank = 6 + (ct % 2)
                ps_wait(bank)
                T_.tag(MM(ps[bank][:, 0:1], nsq6[:, ct * 128:(ct + 1) * 128],
                          onesc[0:64, :], start=True, stop=True))
                V.wait(T_)
                V.wait(V)
                cm = V.tag(nc.vector.tensor_max(sc1[:, :], sc1[:, :],
                                                ps[bank][:, 0:1]))
                ps_done(bank, V, cm)
            bank = 6
            ps_wait(bank)
            T_.wait(V)
            T_.tag(nc.tensor.transpose(ps[bank][0:1, 0:128], sc1[:, 0:1],
                                       ident[:, :]))
            V.wait(T_)
            slot = 4 if jq == 0 else 5
            vm = V.tag(nc.vector.reduce_max(gsc[0:1, slot:slot + 1],
                                            ps[bank][0:1, 0:128], axis=AX.X))
            ps_done(bank, V, vm)
        kq_ready = V.cnt
        V.wait(V)
        V.tag(nc.vector.tensor_mul(gsc[0:1, 0:1], gsc[0:1, 4:5],
                                   gsc[0:1, 5:6]))
        vm = V.cnt
        S.wait(V, vm)
        sm = S.tag(nc.scalar.activation(gsc[0:1, 1:2], gsc[0:1, 0:1], AF.Sqrt))
        bank = 7
        ps_wait(bank)
        T_.wait(S, sm)
        T_.tag(MM(ps[bank][:, 0:1], ones1r[:, :], gsc[0:1, 1:2], start=True,
                  stop=True))
        V.wait(T_)
        vm = V.tag(nc.vector.tensor_scalar_mul(sc2[:, :], ps[bank][:, 0:1],
                                               -0.125 * 1.0001))
        ps_done(bank, V, vm)
        S.wait(V, vm)

        for t in range(NTC):
            for st_ in range(NTT):
                bank = st_ % 2
                ps_wait(bank)
                T_.wait(V, kq_ready)
                T_.tag(MM(ps[bank][0:128, 0:512],
                          kh_bf[:, st_ * 128:(st_ + 1) * 128],
                          qh_bf[:, t * 512:(t + 1) * 512], start=True,
                          stop=True))
                S.wait(T_)
                sm = S.tag(nc.scalar.activation(
                    sl1_bf[:, st_ * 512:(st_ + 1) * 512], ps[bank][:, 0:512],
                    AF.Exp, bias=sc2[:, 0:1],
                    scale=consts[:, C_SC0125:C_SC0125 + 1]))
                ps_done(bank, S, sm)
            bank = 2
            bank2 = 3
            ps_wait(bank)
            ps_wait(bank2)
            T_.wait(S)
            tp = None if par == 0 else (0, 64)
            rows = slice(0, 64) if par == 0 else slice(64, 128)
            for st_ in range(NTT):
                T_.tag(MM(ps[bank][rows, 0:512],
                          v_sb[:, st_ * 1040 + h * 64: st_ * 1040 + h * 64 + 64],
                          sl1_bf[:, st_ * 512:(st_ + 1) * 512],
                          start=(st_ == 0), stop=(st_ == NTT - 1),
                          tile_position=tp))
            for st_ in range(NTT):
                T_.tag(MM(ps[bank2][0:1, 0:512], onesc_b[:, 0:1],
                          sl1_bf[:, st_ * 512:(st_ + 1) * 512],
                          start=(st_ == 0), stop=(st_ == NTT - 1)))
            V.wait(T_)
            vv2 = V.tag(nc.vector.tensor_scalar(tscr2[0:1, 1536:2048],
                                                ps[bank2][0:1, 0:512],
                                                2.0 ** -96, 2.0 ** 26,
                                                op0=ALU.max, op1=ALU.mult))
            ps_done(bank2, V, vv2)
            S.wait(V, vv2)
            sm = S.tag(nc.scalar.activation(tscr2[0:1, 1536:2048],
                                            tscr2[0:1, 1536:2048], AF.Ln))
            S.wait(S)
            sm = S.tag(nc.scalar.activation(tscr2[0:1, 1536:2048],
                                            tscr2[0:1, 1536:2048], AF.Exp,
                                            bias=consts[0:1, C_LN2_26:C_LN2_26 + 1],
                                            scale=consts[0:1, C_NEG1:C_NEG1 + 1]))
            bank3 = 6
            ps_wait(bank3)
            T_.wait(S, sm)
            T_.tag(MM(ps[bank3][:, 0:512], ones1r[:, :], tscr2[0:1, 1536:2048],
                      start=True, stop=True))
            V.wait(T_)
            vm = V.tag(nc.vector.tensor_copy(s1b[:, 0:512],
                                             ps[bank3][:, 0:512]))
            ps_done(bank3, V, vm)
            cm = V.tag(nc.vector.tensor_mul(
                z_sb[rows, pc_ * T + t * 512: pc_ * T + (t + 1) * 512],
                ps[bank][rows, 0:512], s1b[rows, 0:512]))
            ps_done(bank, V, cm)
    z_ready = V.cnt

    for e in range(NCH):
        V.tag(nc.vector.tensor_mul(gab[:, e:e + 1], modv[(1, "ga")][:, e:e + 1],
                                   bout[:, e:e + 1]))
    LS.wait(T_)
    LS.wait(V)
    LS.wait(S)
    LA.wait(T_)
    LA.wait(V)
    LA.wait(S)
    wo1_marks = [0] * NCH
    wo1_cast = [0] * NCH
    wo1_cons = [0] * NCH
    xm1 = [0] * NCH

    def issue_wout(e):
        fb = tscr[:, 0:1024] if e % 2 == 0 else tscr[:, 1024:2048]
        if e >= 2:
            LA.wait(V, wo1_cast[e - 2])
        mk = 0
        for fc in range(NCH):
            mk = load(fb[:, fc * 128:(fc + 1) * 128],
                      wout_h[fc * 128:(fc + 1) * 128, e * 128:(e + 1) * 128],
                      q=LA)
        wo1_marks[e] = mk
        xm1[e] = load(x_sb[:, e * T:(e + 1) * T],
                      xspill_h[:, e * T:(e + 1) * T])

    def cast_wout(e):
        fb = tscr[:, 0:1024] if e % 2 == 0 else tscr[:, 1024:2048]
        pan = wbuf[:, 0:1024] if e % 2 == 0 else wbuf[:, 1024:2048]
        V.wait(LA, wo1_marks[e])
        if e >= 2:
            V.wait(T_, wo1_cons[e - 2])
        wo1_cast[e] = V.tag(nc.vector.tensor_copy(pan, fb))

    issue_wout(0)
    cast_wout(0)
    issue_wout(1)
    cast_wout(1)
    for e in range(NCH):
        pan = wbuf[:, 0:1024] if e % 2 == 0 else wbuf[:, 1024:2048]
        T_.wait(V, wo1_cast[e])
        T_.wait(V, z_ready)
        for t in range(NTC):
            bank = t % 2
            ps_wait(bank)
            for fc in range(NCH):
                T_.tag(MM(ps[bank][:, 0:512], pan[:, fc * 128:(fc + 1) * 128],
                          z_sb[:, fc * T + t * 512: fc * T + (t + 1) * 512],
                          start=(fc == 0), stop=(fc == NCH - 1)))
            V.wait(T_)
            V.wait(LS, xm1[e])
            V.tag(nc.vector.scalar_tensor_tensor(
                x_sb[:, e * T + t * 512: e * T + (t + 1) * 512],
                ps[bank][:, 0:512], modv[(1, "ga")][:, e:e + 1],
                x_sb[:, e * T + t * 512: e * T + (t + 1) * 512],
                op0=ALU.mult, op1=ALU.add))
            cm = V.tag(nc.vector.tensor_scalar(
                x_sb[:, e * T + t * 512: e * T + (t + 1) * 512],
                x_sb[:, e * T + t * 512: e * T + (t + 1) * 512],
                gab[:, e:e + 1], None, op0=ALU.add))
            ps_done(bank, V, cm)
        wo1_cons[e] = T_.cnt
        if e + 2 < NCH:
            issue_wout(e + 2)
            cast_wout(e + 2)
    x_attn1 = V.cnt
    dbg("xa1", x_sb[:, :])
    dbg("z1", z_sb[:, :])

    V.wait(V, x_attn1)
    barrier_CMP()
    V.tag(nc.vector.memset(tscr2[:, 0:2048], 0.0))
    for t in range(NTC):
        ln_prep_t(x_sb, t)
        vm = 0
        for dc in range(NCH):
            vm = ln_u(tscr[:, 0:512], x_sb, dc, t)
            S.wait(V, vm)
            sm = S.tag(nc.scalar.activation(tscr[:, 1024:1536], tscr[:, 0:512],
                                            AF.Identity,
                                            bias=modv[(1, "sb")][:, dc:dc + 1],
                                            scale=modv[(1, "cb1")][:, dc:dc + 1]))
            S.wait(S, sm)
            sm = S.tag(nc.scalar.activation(tscr[:, 1536:2048],
                                            tscr[:, 1024:1536], AF.Abs))
            V.wait(S, sm)
            vm = V.tag(nc.vector.tensor_max(tscr2[:, t * 512:(t + 1) * 512],
                                            tscr2[:, t * 512:(t + 1) * 512],
                                            tscr[:, 1536:2048]))
        ps_done(4, V, vm)
        ps_done(5, V, vm)
    T_.wait(V)
    for tt in range(NTT):
        bank = 6 + (tt % 2)
        ps_wait(bank)
        T_.tag(nc.tensor.transpose(ps[bank][:, 0:128],
                                   tscr2[:, tt * 128:(tt + 1) * 128],
                                   ident[:, :]))
        V.wait(T_)
        cm = V.tag(nc.vector.reduce_max(tmax[:, tt:tt + 1],
                                        ps[bank][:, 0:128], axis=AX.X))
        ps_done(bank, V, cm)
    V.wait(V)
    vm = V.tag(nc.vector.tensor_single_scalar(kn2c[:, :], tmax[:, :], 1e-5,
                                              op=ALU.max))
    S.wait(V, vm)
    S.tag(nc.scalar.activation(racc[:, :], kn2c[:, :], AF.Ln))
    S.wait(S)
    sm = S.tag(nc.scalar.activation(racc[:, :], racc[:, :], AF.Exp,
                                    bias=consts[:, C_LN127:C_LN127 + 1],
                                    scale=consts[:, C_NEG1:C_NEG1 + 1]))
    V.wait(S, sm)
    vm = V.tag(nc.vector.tensor_scalar_mul(rinv[:, :], kn2c[:, :],
                                           1.0 / 127.0))
    T_.wait(V, vm)
    for tt in range(NTT):
        bank = 6 + (tt % 2)
        ps_wait(bank)
        T_.tag(nc.tensor.transpose(ps[bank][0:1, 0:128], racc[:, tt:tt + 1],
                                   ident[:, :]))
        V.wait(T_)
        cm = V.tag(nc.vector.tensor_copy(r1row[0:1, tt * 128:(tt + 1) * 128],
                                         ps[bank][0:1, 0:128]))
        ps_done(bank, V, cm)
    for tt in range(NTT):
        bank = 6 + (tt % 2)
        ps_wait(bank)
        T_.tag(nc.tensor.transpose(ps[bank][0:1, 0:128], rinv[:, tt:tt + 1],
                                   ident[:, :]))
        V.wait(T_)
        cm = V.tag(nc.vector.tensor_copy(tscr2[0:1, tt * 128:(tt + 1) * 128],
                                         ps[bank][0:1, 0:128]))
        ps_done(bank, V, cm)
    vm = V.tag(nc.vector.tensor_scalar(tscr2[0:1, 0:2048], tscr2[0:1, 0:2048],
                                       swsc[0:1, 0:1], None, op0=ALU.mult))
    s1_ready = vm
    r1_ready = vm

    for t in range(NTC):
        bank_r = 3
        ps_wait(bank_r)
        T_.wait(V, r1_ready)
        for q_ in range(4):
            T_.tag(MM(ps[bank_r][:, q_ * 128:(q_ + 1) * 128], ones1r[:, :],
                      r1row[0:1, (t * 4 + q_) * 128: (t * 4 + q_ + 1) * 128],
                      start=(q_ == 0), stop=(q_ == 3)))
        ln_prep_t(x_sb, t)
        vm = 0
        for dc in range(NCH):
            vm = ln_u(tscr[:, 0:512], x_sb, dc, t)
            S.wait(V, vm)
            sm = S.tag(nc.scalar.activation(tscr[:, 1024:1536], tscr[:, 0:512],
                                            AF.Identity,
                                            bias=modv[(1, "sb")][:, dc:dc + 1],
                                            scale=modv[(1, "cb1")][:, dc:dc + 1]))
            V.wait(S, sm)
            V.tag(nc.vector.tensor_mul(tscr[:, 1536:2048], tscr[:, 1024:1536],
                                       ps[bank_r][:, 0:512]))
            V.tag(nc.vector.tensor_scalar(tscr[:, 1536:2048],
                                          tscr[:, 1536:2048], -128.0, 127.0,
                                          op0=ALU.max, op1=ALU.min))
            vm = V.tag(nc.vector.tensor_scalar(
                hb[:, dc * T + t * 512: dc * T + (t + 1) * 512],
                tscr[:, 1536:2048], MAGIC, MAGIC, op0=ALU.add,
                op1=ALU.subtract))
        ps_done(4, V, vm)
        ps_done(5, V, vm)
        ps_done(bank_r, V, vm)
    iq1_ready = V.cnt

    for t in range(NTC):
        bank = 6 + (t % 2)
        ps_wait(bank)
        T_.wait(V, s1_ready)
        for q_ in range(4):
            T_.tag(MM(ps[bank][:, q_ * 128:(q_ + 1) * 128], ones1r[:, :],
                      tscr2[0:1, (t * 4 + q_) * 128:(t * 4 + q_ + 1) * 128],
                      start=(q_ == 0), stop=(q_ == 3)))
        V.wait(T_)
        cm = V.tag(nc.vector.tensor_copy(s1_all[:, t * 512:(t + 1) * 512],
                                         ps[bank][:, 0:512]))
        ps_done(bank, V, cm)
    V.tag(nc.vector.memset(ymax_acc[:, :], 0.0))
    s1a_ready = V.cnt

    NSPW = NI1 // 4
    w1_marks = [0] * NSPW
    w1_cons = [0] * NSPW

    def issue_spw1(sp):
        if sp >= 2:
            LS.wait(T_, w1_cons[sp - 2])
        mk = 0
        for dc in range(NCH):
            mk = loadT(spw1[sp % 2][:, dc * 512:(dc + 1) * 512],
                       w1_b[sp * 512:(sp + 1) * 512, dc * 128:(dc + 1) * 128])
        w1_marks[sp] = mk

    LS.wait(T_)
    LS.wait(V)
    LS.wait(S)
    issue_spw1(0)
    issue_spw1(1)
    for ic in range(8):
        dstb = (w2z2[:, ic * 1024:(ic + 1) * 1024] if ic < 4
                else w2v[:, (ic - 4) * 1024:(ic - 3) * 1024])
        loadT(dstb, w2_b[0:D, ic * 128:(ic + 1) * 128])

    T_.wait(V, iq1_ready)
    st7 = {}
    abs7 = {}
    for sp in range(NSPW):
        buf = spw1[sp % 2]
        T_.wait(LS, w1_marks[sp])
        for ii in range(4):
            ic = sp * 4 + ii
            h1r = tscr if ic % 2 == 0 else h1rB
            for t in range(NTC):
                ps_wait(t)
                for dc in range(NCH):
                    T_.tag(MM(ps[t][:, 0:512],
                              buf[:, dc * 512 + ii * 128:
                                  dc * 512 + ii * 128 + 128],
                              hb[:, dc * T + t * 512: dc * T + (t + 1) * 512],
                              start=(dc == 0), stop=(dc == NCH - 1)))
            if ic >= 2:
                V.wait(LG, st7[ic - 2])
            V.wait(V, s1a_ready)
            V.wait(T_)
            for t in range(NTC):
                vm = V.tag(nc.vector.tensor_mul(
                    h1r[:, t * 512:(t + 1) * 512], ps[t][:, 0:512],
                    s1_all[:, t * 512:(t + 1) * 512]))
                ps_done(t, V, vm)
            if ic >= 1:
                V.wait(S, abs7[ic - 1])
            V.tag(nc.vector.tensor_scalar(tscr2[:, 0:T], h1r[:, 0:T],
                                          convw[:, ic * 3 + 1: ic * 3 + 2],
                                          None, op0=ALU.mult))
            V.tag(nc.vector.scalar_tensor_tensor(
                tscr2[:, 1:T], h1r[:, 0:T - 1],
                convw[:, ic * 3 + 0: ic * 3 + 1], tscr2[:, 1:T],
                op0=ALU.mult, op1=ALU.add))
            cv = V.tag(nc.vector.scalar_tensor_tensor(
                tscr2[:, 0:T - 1], h1r[:, 1:T],
                convw[:, ic * 3 + 2: ic * 3 + 3], tscr2[:, 0:T - 1],
                op0=ALU.mult, op1=ALU.add))
            S.wait(V, cv)
            sm = S.tag(nc.scalar.activation(h1r[:, 0:T], tscr2[:, 0:T],
                                            AF.Silu, bias=convb[:, ic:ic + 1],
                                            scale=1.0))
            S.wait(S)
            am = S.tag(nc.scalar.activation(tscr2[:, 0:T], h1r[:, 0:T],
                                            AF.Abs))
            abs7[ic] = am
            V.wait(S, am)
            V.tag(nc.vector.tensor_max(ymax_acc[:, 0:T], ymax_acc[:, 0:T],
                                       tscr2[:, 0:T]))
            G.wait(S, sm)
            st7[ic] = store(y_h[ic * 128:(ic + 1) * 128, 0:T], h1r[:, 0:T])
        w1_cons[sp] = T_.cnt
        if sp + 2 < NSPW:
            issue_spw1(sp + 2)
    w1_T_done = T_.cnt
    dbg("ymaxd", ymax_acc[:, :])
    dbg("y31", h1rB[:, :])
    dbg("s1alld", s1_all[:, :])

    LS.wait(T_, w1_T_done)
    for ic in range(8, NI1):
        dstb = (w2h[:, (ic - 8) * 1024:(ic - 7) * 1024] if ic < 24
                else w2z[:, (ic - 24) * 1024:(ic - 23) * 1024])
        loadT(dstb, w2_b[0:D, ic * 128:(ic + 1) * 128])
    w2_loaded = LS.cnt

    T_.wait(V)
    for tt in range(NTT):
        bank = 6 + (tt % 2)
        ps_wait(bank)
        T_.tag(nc.tensor.transpose(ps[bank][:, 0:128],
                                   ymax_acc[:, tt * 128:(tt + 1) * 128],
                                   ident[:, :]))
        V.wait(T_)
        cm = V.tag(nc.vector.reduce_max(tmax[:, tt:tt + 1],
                                        ps[bank][:, 0:128], axis=AX.X))
        ps_done(bank, V, cm)
    V.wait(V)
    vm = V.tag(nc.vector.tensor_single_scalar(kn2c[:, :], tmax[:, :], 1e-5,
                                              op=ALU.max))
    S.wait(V, vm)
    S.tag(nc.scalar.activation(racc[:, :], kn2c[:, :], AF.Ln))
    S.wait(S)
    sm = S.tag(nc.scalar.activation(racc[:, :], racc[:, :], AF.Exp,
                                    bias=consts[:, C_LN127:C_LN127 + 1],
                                    scale=consts[:, C_NEG1:C_NEG1 + 1]))
    V.wait(S, sm)
    vm = V.tag(nc.vector.tensor_scalar_mul(rinv[:, :], kn2c[:, :],
                                           1.0 / 127.0))
    T_.wait(V, vm)
    for tt in range(NTT):
        bank = 6 + (tt % 2)
        ps_wait(bank)
        T_.tag(nc.tensor.transpose(ps[bank][0:1, 0:128], racc[:, tt:tt + 1],
                                   ident[:, :]))
        V.wait(T_)
        cm = V.tag(nc.vector.tensor_copy(r1row[0:1, tt * 128:(tt + 1) * 128],
                                         ps[bank][0:1, 0:128]))
        ps_done(bank, V, cm)
    for tt in range(NTT):
        bank = 6 + (tt % 2)
        ps_wait(bank)
        T_.tag(nc.tensor.transpose(ps[bank][0:1, 0:128], rinv[:, tt:tt + 1],
                                   ident[:, :]))
        V.wait(T_)
        cm = V.tag(nc.vector.tensor_copy(tscr2[0:1, tt * 128:(tt + 1) * 128],
                                         ps[bank][0:1, 0:128]))
        ps_done(bank, V, cm)
    vm = V.tag(nc.vector.tensor_scalar(tscr2[0:1, 0:2048], tscr2[0:1, 0:2048],
                                       swsc[0:1, 1:2], None, op0=ALU.mult))
    r2_ready = vm

    for t in range(NTC):
        bank = 6 + (t % 2)
        ps_wait(bank)
        T_.wait(V, r2_ready)
        for q_ in range(4):
            T_.tag(MM(ps[bank][:, q_ * 128:(q_ + 1) * 128], ones1r[:, :],
                      r1row[0:1, (t * 4 + q_) * 128:(t * 4 + q_ + 1) * 128],
                      start=(q_ == 0), stop=(q_ == 3)))
        V.wait(T_)
        cm = V.tag(nc.vector.tensor_copy(rs_all[:, t * 512:(t + 1) * 512],
                                         ps[bank][:, 0:512]))
        ps_done(bank, V, cm)
    for t in range(NTC):
        bank = 6 + (t % 2)
        ps_wait(bank)
        for q_ in range(4):
            T_.tag(MM(ps[bank][:, q_ * 128:(q_ + 1) * 128], ones1r[:, :],
                      tscr2[0:1, (t * 4 + q_) * 128:(t * 4 + q_ + 1) * 128],
                      start=(q_ == 0), stop=(q_ == 3)))
        V.wait(T_)
        cm = V.tag(nc.vector.tensor_copy(ss_all[:, t * 512:(t + 1) * 512],
                                         ps[bank][:, 0:512]))
        ps_done(bank, V, cm)
    rsss_ready = V.cnt
    dbg("rsd", rs_all[:, :])
    dbg("ssd", ss_all[:, :])

    def w2_stripe(ic):
        if ic < 4:
            return w2z2[:, ic * 1024:(ic + 1) * 1024]
        if ic < 8:
            return w2v[:, (ic - 4) * 1024:(ic - 3) * 1024]
        if ic < 24:
            return w2h[:, (ic - 8) * 1024:(ic - 7) * 1024]
        return w2z[:, (ic - 24) * 1024:(ic - 23) * 1024]

    LA.wait(LG)
    V.wait(LG)
    itw = 0
    qm_w = {}
    cons_w = {}
    for t in range(NTC):
        for hc in range(NCH):
            ps_wait(hc)
        for ic in range(NI1):
            sl = itw % 2
            if itw >= 2:
                LA.wait(V, qm_w[itw - 2])
            mk = load(yq[sl][:, 0:512],
                      y_h[ic * 128:(ic + 1) * 128, t * 512:(t + 1) * 512],
                      q=LA)
            V.wait(LA, mk)
            V.wait(V, rsss_ready)
            if itw >= 2:
                V.wait(T_, cons_w[itw - 2])
            V.tag(nc.vector.tensor_mul(tscr[:, sl * 512:sl * 512 + 512],
                                       yq[sl][:, 0:512],
                                       rs_all[:, t * 512:(t + 1) * 512]))
            V.tag(nc.vector.tensor_scalar(tscr[:, sl * 512:sl * 512 + 512],
                                          tscr[:, sl * 512:sl * 512 + 512],
                                          -128.0, 127.0, op0=ALU.max,
                                          op1=ALU.min))
            qm = V.tag(nc.vector.tensor_scalar(
                iqs[sl][:, 0:512], tscr[:, sl * 512:sl * 512 + 512], MAGIC,
                MAGIC, op0=ALU.add, op1=ALU.subtract))
            qm_w[itw] = qm
            T_.wait(LS, w2_loaded)
            T_.wait(V, qm)
            for hc in range(NCH):
                T_.tag(MM(ps[hc][:, 0:512],
                          w2_stripe(ic)[:, hc * 128:(hc + 1) * 128],
                          iqs[sl][:, 0:512], start=(ic == 0),
                          stop=(ic == NI1 - 1)))
            cons_w[itw] = T_.cnt
            itw += 1
        V.wait(T_)
        for hc in range(NCH):
            V.tag(nc.vector.tensor_mul(tscr[:, 1024:1536], ps[hc][:, 0:512],
                                       ss_all[:, t * 512:(t + 1) * 512]))
            cm = V.tag(nc.vector.scalar_tensor_tensor(
                x_sb[:, hc * T + t * 512: hc * T + (t + 1) * 512],
                tscr[:, 1024:1536], modv[(1, "gb")][:, hc:hc + 1],
                x_sb[:, hc * T + t * 512: hc * T + (t + 1) * 512],
                op0=ALU.mult, op1=ALU.add))
            ps_done(hc, V, cm)
    x_final = V.cnt
    dbg("xfin", x_sb[:, :])
    dbg("iq1", hb[:, :])

    T_.wait(V, x_final)
    smarks = []
    for tt in range(NTT):
        ob = outsb[:, 0:1024] if tt % 2 == 0 else outsb[:, 1024:2048]
        if tt >= 2:
            V.wait(LG, smarks[tt - 2])
        for dc in range(NCH):
            bank = 4 + (dc % 2)
            ps_wait(bank)
            T_.tag(nc.tensor.transpose(
                ps[bank][:, 0:128],
                x_sb[:, dc * T + tt * 128: dc * T + (tt + 1) * 128],
                ident[:, :]))
            V.wait(T_)
            cm = V.tag(nc.vector.tensor_copy(ob[:, dc * 128:(dc + 1) * 128],
                                             ps[bank][:, 0:128]))
            ps_done(bank, V, cm)
        G.wait(V)
        smarks.append(store(out_h[tt * 128:(tt + 1) * 128, :], ob))
    nc.gpsimd.wait_ge(LG.sem, LG.cnt)
    return nc


_NC_CACHE = None


def kernel(**ins):
    global _NC_CACHE
    if _NC_CACHE is None:
        _NC_CACHE = build()
    nc = _NC_CACHE

    f32 = np.float32
    x = np.ascontiguousarray(ins["L0_attn_x"], f32)
    half = 32
    freqs = 1.0 / 500.0 ** (np.arange(half, dtype=np.float64) / half)
    ang = np.arange(T, dtype=np.float64)[None, :] * freqs[:, None]
    cosv = np.cos(ang).astype(f32)
    sinv = np.sin(ang).astype(f32)
    cs64 = np.ascontiguousarray(np.concatenate([cosv, cosv], axis=0))
    sn64 = np.ascontiguousarray(np.concatenate([-sinv, sinv], axis=0))
    identb = np.eye(128, dtype=f32)
    constsb = np.zeros((128, NCONST), f32)
    constsb[:, C_LN127] = np.log(np.float32(127.0))
    constsb[:, C_SC0125] = 0.125
    constsb[:, C_NEG1] = -1.0
    constsb[:, C_NEGH] = -0.5
    constsb[:, C_LN2_26] = np.float32(26 * np.log(2.0))
    dmaskb = np.where(np.arange(128)[:, None] >= np.arange(128)[None, :],
                      np.float32(0.0), np.float32(-1e9)).astype(f32)

    wqkv = np.ascontiguousarray(ins["L1_attn_w_qkv"], f32)
    bqkv = np.ascontiguousarray(ins["L1_attn_b_qkv"], f32)
    swp = np.concatenate([np.arange(32, 64), np.arange(0, 32)])
    blocks = []
    bias_cols = []
    for h in range(H):
        qcols = wqkv[:, h * 64:(h + 1) * 64]
        kcols = wqkv[:, D + h * 64: D + (h + 1) * 64]
        blocks += [qcols, qcols[:, swp], kcols, kcols[:, swp]]
        bias_cols += [bqkv[h * 64:(h + 1) * 64],
                      bqkv[h * 64:(h + 1) * 64][swp],
                      bqkv[D + h * 64: D + (h + 1) * 64],
                      bqkv[D + h * 64: D + (h + 1) * 64][swp]]
    wqkvr = np.ascontiguousarray(np.concatenate(blocks, axis=1))
    biasr = np.ascontiguousarray(np.stack(bias_cols, axis=1))
    wv1 = np.ascontiguousarray(wqkv[:, 2 * D:3 * D])
    bvT = np.ascontiguousarray(bqkv[2 * D:3 * D].reshape(8, 128).T)

    common = {
        "wk": np.ascontiguousarray(ins["L0_attn_W_K"].reshape(D, D), f32),
        "wq": np.ascontiguousarray(ins["L0_attn_W_Q"].reshape(D, D), f32),
        "wv": np.ascontiguousarray(ins["L0_attn_W_V"].reshape(D, D), f32),
        "wo": np.ascontiguousarray(ins["L0_attn_W_O"], f32),
        "wg": np.ascontiguousarray(ins["L0_ffn_w_gate"], f32),
        "wu": np.ascontiguousarray(ins["L0_ffn_w_up"], f32),
        "wd": np.ascontiguousarray(ins["L0_ffn_w_down"], f32),
        "wmod0": np.ascontiguousarray(ins["L0_wmod"], f32),
        "wmod1": np.ascontiguousarray(ins["L1_wmod"], f32),
        "wqkvr": wqkvr,
        "wv1": wv1,
        "biasr_in": biasr,
        "bvT": bvT,
        "wout": np.ascontiguousarray(ins["L1_attn_w_out"], f32),
        "boutT": np.ascontiguousarray(
            ins["L1_attn_b_out"].reshape(8, 128).T, f32),
        "w1": np.ascontiguousarray(ins["L1_ffn_w1"], f32),
        "w2": np.ascontiguousarray(ins["L1_ffn_w2"], f32),
        "convwT": np.ascontiguousarray(
            ins["L1_ffn_conv_w"].reshape(NI1, 128, 3).transpose(1, 0, 2)
            .reshape(128, NI1 * 3), f32),
        "convbT": np.ascontiguousarray(
            ins["L1_ffn_conv_b"].reshape(NI1, 128).T, f32),
        "clipx": np.ascontiguousarray(ins["L0_ffn_clip_x"].reshape(1, 1), f32),
        "clipd": np.ascontiguousarray(ins["L0_ffn_clip_down"].reshape(1, 1),
                                      f32),
        "identh": identb,
        "constsh": constsb,
        "cs64h": cs64,
        "sn64h": sn64,
        "dmaskh": dmaskb,
    }
    in_maps = []
    for b in range(8):
        m = dict(common)
        m["x"] = np.ascontiguousarray(x[b])
        m["cb0"] = np.ascontiguousarray(ins["L0_c"][b:b + 1], f32)
        m["cb1"] = np.ascontiguousarray(ins["L1_c"][b:b + 1], f32)
        in_maps.append(m)

    res = run_bass_kernel_spmd(nc, in_maps, core_ids=list(range(8)))
    return np.stack([res.results[b]["out"] for b in range(8)], axis=0)
